# version 13
# baseline (speedup 1.0000x reference)
import contextlib
import os
import numpy as np
import ml_dtypes
import concourse.bass as bass
import concourse.mybir as mybir
from concourse.bass_utils import run_bass_kernel_spmd

F32 = mybir.dt.float32
BF16 = mybir.dt.bfloat16
AF = mybir.ActivationFunctionType
ALU = mybir.AluOpType
AX = mybir.AxisListType

T = 2048
D = 2048
H = 16
HD = 128
BLK = 256
NBLK = T // BLK
DFF = 8192
NC = 8
MAGIC = 12582912.0
EPS_LN = 1e-5
EPS_RMS = 1e-6
EPS_QA = 1e-5

MT = T // 128
MD = D // 128
MF = DFF // 128


def build_nc(debug=False, upto=99):
    nc = bass.Bass("TRN2", target_bir_lowering=False)
    ctx = contextlib.ExitStack()

    x_in = nc.dram_tensor("x_in", [T, D], F32, kind="ExternalInput").ap()
    wqkT = nc.dram_tensor("wqkT", [D, 2 * D], F32, kind="ExternalInput").ap()
    wvuT = nc.dram_tensor("wvuT", [D, 2 * D], F32, kind="ExternalInput").ap()
    woT = nc.dram_tensor("woT", [D, D], F32, kind="ExternalInput").ap()
    w1T = nc.dram_tensor("w1T", [D, DFF], F32, kind="ExternalInput").ap()
    w2T = nc.dram_tensor("w2T", [D, DFF], F32, kind="ExternalInput").ap()
    w3T = nc.dram_tensor("w3T", [DFF, D], F32, kind="ExternalInput").ap()
    slope_b = nc.dram_tensor("slope_b", [128, H], F32, kind="ExternalInput").ap()
    kvmask = nc.dram_tensor("kvmask", [128, NC], F32, kind="ExternalInput").ap()
    out = nc.dram_tensor("out", [T, D], F32, kind="ExternalOutput").ap()

    dkind = "ExternalOutput" if debug else "Internal"
    D_xn1 = nc.dram_tensor("D_xn1", [T, D], BF16, kind=dkind).ap()
    D_qfm = nc.dram_tensor("D_qfm", [D, T], BF16, kind=dkind).ap()
    D_kfm = nc.dram_tensor("D_kfm", [D, T], BF16, kind=dkind).ap()
    D_v = nc.dram_tensor("D_v", [T, D], BF16, kind=dkind).ap()
    D_u = nc.dram_tensor("D_u", [T, D], BF16, kind=dkind).ap()
    D_z = nc.dram_tensor("D_z", [T, D], BF16, kind=dkind).ap()
    D_zT = nc.dram_tensor("D_zT", [D, T], BF16, kind=dkind).ap()
    D_x2 = nc.dram_tensor("D_x2", [T, D], F32, kind=dkind).ap()
    D_qayT = nc.dram_tensor("D_qayT", [D, T], BF16, kind=dkind).ap()
    D_qah = nc.dram_tensor("D_qah", [T, DFF], BF16, kind=dkind).ap()
    D_qahT = nc.dram_tensor("D_qahT", [DFF, T], BF16, kind=dkind).ap()
    D_pre4 = nc.dram_tensor("D_pre4", [T, D], BF16, kind=dkind).ap()
    D_w1s = nc.dram_tensor("D_w1s", [D, DFF], BF16).ap()
    D_w2s = nc.dram_tensor("D_w2s", [D, DFF], BF16).ap()
    D_w3s = nc.dram_tensor("D_w3s", [DFF, D], BF16).ap()
    D_kv0 = nc.dram_tensor("D_kv0", [128, H * HD], F32, kind=dkind).ap()
    D_cci = nc.dram_tensor("D_cci", [H * HD, HD], F32)
    D_cco = nc.dram_tensor("D_cco", [NC * H * HD, HD], F32)

    ident_t = nc.inline_tensor(np.eye(128, dtype=ml_dtypes.bfloat16), "ident")
    ones_col_t = nc.inline_tensor(np.ones((128, 1), np.float32), "ones_col")
    ones_row_t = nc.inline_tensor(np.ones((1, 128), np.float32), "ones_row")
    m_ar = np.arange(BLK, dtype=np.float32)
    dargs = np.stack([m_ar + 1.0, 255.0 - m_ar, np.full(BLK, 256.0, np.float32)], axis=1)
    dargs3_t = nc.inline_tensor(np.ascontiguousarray(dargs), "dargs3")
    idx = m_ar[None, :] - m_ar[:, None]
    idxT = np.where(idx >= 0.0, idx, 1e9).astype(np.float32)
    idxT_t = nc.inline_tensor(np.ascontiguousarray(idxT), "idxT")
    posr_t = nc.inline_tensor(np.ascontiguousarray(np.broadcast_to(np.arange(1, 257, dtype=np.float32), (128, 256))), "posr")

    class S:
        def __init__(self, name):
            self.h = ctx.enter_context(nc.semaphore(name))
            self.n = 0

    class Pool:
        def __init__(self, name, n=8):
            self.sems = [S(f"{name}{i}") for i in range(n)]
            self.i = 0

    SXP = Pool("sx")
    GXP = Pool("gx")
    cc = S("cc")
    tm = S("tm")
    vm = S("vm")
    am = S("am")
    pm = S("pm")
    bar = S("bar")

    blk = ctx.enter_context(nc.Block())
    sb = nc.sbuf_tensor

    def t_inc(instr, s, k=1):
        instr.then_inc(s.h, k)
        s.n += k
        return s.n

    def dma16(eng, pool, **kw):
        s_ = pool.sems[pool.i % len(pool.sems)]
        pool.i += 1
        if s_.n >= 16:
            eng.wait_ge(s_.h, s_.n)
        ins = eng.dma_start(**kw)
        ins.then_inc(s_.h, 16)
        s_.n += 16
        return (s_.h, s_.n)

    def snap(pool):
        return [(s_.h, s_.n) for s_ in pool.sems]

    def wt(eng, tok):
        if isinstance(tok, list):
            for hh, vv in tok:
                eng.wait_ge(hh, vv)
        else:
            eng.wait_ge(tok[0], tok[1])

    sd_ops = []

    def sadd(fn, ninc=1):
        am.n += ninc
        sd_ops.append(fn)
        return am.n

    def sflush():
        if not sd_ops:
            return
        ops = list(sd_ops)
        sd_ops.clear()

        @blk.scalar
        def _(s):
            for f in ops:
                f(s)

    def barrier():
        sflush()
        snaps = snap(SXP) + snap(GXP) + [(tm.h, tm.n), (vm.h, vm.n), (am.h, am.n)]

        def wait_all(e):
            for hh, vv in snaps:
                e.wait_ge(hh, vv)

        @blk.sync
        def _(sy):
            wait_all(sy)

        @blk.gpsimd
        def _(g):
            wait_all(g)

        @blk.tensor
        def _(t):
            wait_all(t)

        @blk.vector
        def _(v):
            wait_all(v)

        @blk.scalar
        def _(s):
            wait_all(s)

    ident = ctx.enter_context(sb("ident_s", [128, 128], BF16))
    ones_col = ctx.enter_context(sb("ones_col_s", [128, 1], F32))
    ones_row = ctx.enter_context(sb("ones_row_s", [1, 128], F32))
    slope_s = ctx.enter_context(sb("slope_s", [128, H], F32))
    nslope = ctx.enter_context(sb("nslope", [128, H], F32))
    kvmask_s = ctx.enter_context(sb("kvmask_s", [128, NC], F32))
    eA = contextlib.ExitStack()
    dargs3 = eA.enter_context(sb("dargs3_s", [128, 2, 3], F32, side="right"))
    idxT_s = eA.enter_context(sb("idxT_s", [128, 2, 256], F32, side="right"))
    qdec = eA.enter_context(sb("qdec", [128, 2, H], F32, side="right"))
    kdec = eA.enter_context(sb("kdec", [128, 2, H], F32, side="right"))
    bdec = eA.enter_context(sb("bdec", [128, H], F32, side="right"))
    decayT = eA.enter_context(sb("decayT", [128, H, 2, 256], BF16, side="right"))
    posr_s = eA.enter_context(sb("posr_s", [128, 256], F32, side="right"))
    qdecT = eA.enter_context(sb("qdecT", [128, H, 256], BF16, side="right"))
    wscale = ctx.enter_context(sb("wscale", [128, 4], F32))
    wabs = ctx.enter_context(sb("wabs", [128, 64], F32))
    scal_1 = ctx.enter_context(sb("scal_1", [1, 1], F32))
    ln4st = ctx.enter_context(sb("ln4st", [128, 128], F32))
    epsc = ctx.enter_context(sb("epsc", [128, 4], F32))

    @blk.sync
    def _(sy):
        for dst, src in [(ident[:, :], ident_t.ap()[:, :]),
                         (ones_col[:, :], ones_col_t.ap()[:, :]),
                         (ones_row[:, :], ones_row_t.ap()[:, :]),
                         (slope_s[:, :], slope_b[:, :]),
                         (kvmask_s[:, :], kvmask[:, :])]:
            dma16(sy, SXP, out=dst, in_=src)
        dma16(sy, SXP, out=dargs3[:, :, :], in_=dargs3_t.ap().rearrange("(c p) j -> p c j", p=128))
        dma16(sy, SXP, out=idxT_s[:, :, :], in_=idxT_t.ap().rearrange("(c p) m -> p c m", p=128))
        dma16(sy, SXP, out=posr_s[:, :], in_=posr_t.ap()[:, :])
    prep_loads = snap(SXP)

    @blk.vector
    def _(v):
        wt(v, prep_loads)
        v.tensor_scalar_mul(out=nslope[:, :], in0=slope_s[:, :], scalar1=-1.0)
        v.memset(epsc[:, 0:1], EPS_LN)
        v.memset(epsc[:, 1:2], EPS_RMS)
        v.memset(epsc[:, 2:3], -0.5)
        v.drain()
        for c in range(2):
            v.tensor_tensor(out=qdec[:, c, :], in0=nslope[:, :], in1=dargs3[:, c, 0:1].to_broadcast([128, H]), op=ALU.mult)
            v.tensor_tensor(out=kdec[:, c, :], in0=nslope[:, :], in1=dargs3[:, c, 1:2].to_broadcast([128, H]), op=ALU.mult)
        v.tensor_tensor(out=bdec[:, :], in0=nslope[:, :], in1=dargs3[:, 0, 2:3].to_broadcast([128, H]), op=ALU.mult)
        for h in range(H):
            for c in range(2):
                v.tensor_scalar(out=decayT[:, h, c, :], in0=idxT_s[:, c, :], scalar1=nslope[:, h:h + 1], scalar2=None, op0=ALU.mult)
            v.tensor_scalar(out=qdecT[:, h, :], in0=posr_s[:, :], scalar1=nslope[:, h:h + 1], scalar2=None, op0=ALU.mult)
        t_inc(v.engine_nop(), vm)
    prep_v = vm.n

    @blk.scalar
    def _(s):
        s.wait_ge(vm.h, prep_v)
        for c in range(2):
            s.activation(out=qdec[:, c, :], in_=qdec[:, c, :], func=AF.Exp)
            s.activation(out=kdec[:, c, :], in_=kdec[:, c, :], func=AF.Exp)
        s.activation(out=bdec[:, :], in_=bdec[:, :], func=AF.Exp)
        last = None
        for h in range(H):
            for c in range(2):
                last = s.activation(out=decayT[:, h, c, :], in_=decayT[:, h, c, :], func=AF.Exp)
            last = s.activation(out=qdecT[:, h, :], in_=qdecT[:, h, :], func=AF.Exp)
        t_inc(last, am)
    prep_done = am.n

    def weight_prep_make(estack, name, src_ap, dst_ap, rows, cols, colw, acc_off, nslot=3):
        nrow = rows // 128
        ncsp = cols // colw
        nchunk = nrow * ncsp
        b_in = estack.enter_context(sb(f"wp_in_{name}", [128, nslot, colw], BF16))
        b_sg = estack.enter_context(sb(f"wp_sg_{name}", [128, 2, colw], BF16))
        t_in, t_sg, t_abs, t_spill = {}, {}, {}, {}

        def rng(kc):
            return (kc // ncsp) * 128, (kc % ncsp) * colw

        def emit_dma(kc):
            if kc >= nchunk:
                return

            @blk.gpsimd
            def _(g):
                if kc >= nslot:
                    g.wait_ge(am.h, t_sg[kc - nslot])
                    g.wait_ge(vm.h, t_abs[kc - nslot])
                r, c0 = rng(kc)
                t_in[kc] = dma16(g, GXP, out=b_in[:, kc % nslot, :], in_=src_ap[r:r + 128, c0:c0 + colw])

        def emit_sign(kc):
            tin = t_in[kc]
            tsp = t_spill.get(kc - 2)

            def f(s, kc=kc, tin=tin, tsp=tsp):
                wt(s, tin)
                if tsp is not None:
                    wt(s, tsp)
                s.activation(out=b_sg[:, kc % 2, :], in_=b_in[:, kc % nslot, :], func=AF.Sign).then_inc(am.h, 1)
            t_sg[kc] = sadd(f)

        def emit_abs(kc):
            @blk.vector
            def _(v):
                wt(v, t_in[kc])
                t_abs[kc] = t_inc(v.tensor_reduce(out=wabs[:, acc_off + kc:acc_off + kc + 1], in_=b_in[:, kc % nslot, :],
                                                  axis=AX.X, op=ALU.add, apply_absolute_value=True), vm)

        def emit_spill(kc):
            @blk.sync
            def _(sy):
                sy.wait_ge(am.h, t_sg[kc])
                r, c0 = rng(kc)
                t_spill[kc] = dma16(sy, SXP, out=dst_ap[r:r + 128, c0:c0 + colw], in_=b_sg[:, kc % 2, :])

        def emit_chunk(kc):
            if kc >= nchunk:
                return
            emit_sign(kc)
            emit_abs(kc)
            emit_spill(kc)
            emit_dma(kc + nslot)

        return {"emit_dma": emit_dma, "emit_chunk": emit_chunk, "nchunk": nchunk, "nslot": nslot,
                "abs_last": lambda: t_abs[nchunk - 1]}

    def finish_scale(n_elems, out_col, abs_tok, nchunk, acc_off, psA, psB, tmp_col, pe_wait_vm=0, pe_wait_am=0):
        @blk.vector
        def _(v):
            v.wait_ge(vm.h, abs_tok)
            v.tensor_reduce(out=tmp_col, in_=wabs[:, acc_off:acc_off + nchunk], axis=AX.X, op=ALU.add)
            t_inc(v.engine_nop(), vm)
        t0 = vm.n

        @blk.tensor
        def _(t):
            t.wait_ge(vm.h, t0)
            if pe_wait_vm:
                t.wait_ge(vm.h, pe_wait_vm)
            if pe_wait_am:
                t.wait_ge(am.h, pe_wait_am)
            t_inc(t.matmul(psA, ones_col[:, :], tmp_col, start=True, stop=True), tm)
        t1 = tm.n

        @blk.vector
        def _(v):
            v.wait_ge(tm.h, t1)
            t_inc(v.tensor_copy(out=scal_1[:, :], in_=psA), vm)
        t2 = vm.n

        @blk.tensor
        def _(t):
            t.wait_ge(vm.h, t2)
            t_inc(t.matmul(psB, ones_row[:, :], scal_1[:, :], start=True, stop=True), tm)
        t3 = tm.n

        @blk.vector
        def _(v):
            v.wait_ge(tm.h, t3)
            t_inc(v.tensor_scalar_mul(out=wscale[:, out_col:out_col + 1], in0=psB, scalar1=1.0 / float(n_elems)), vm)
        return vm.n

    e1 = contextlib.ExitStack()
    x_buf = e1.enter_context(sb("x_buf", [128, 2, D], F32))
    xn1_bf = e1.enter_context(sb("xn1_bf", [128, 2, D], BF16))
    stat = e1.enter_context(sb("stat", [128, 16], F32))
    dump = e1.enter_context(sb("dump", [128, D], F32))
    xn1T = e1.enter_context(sb("xn1T", [128, MD * T], BF16))
    wqk_b = e1.enter_context(sb("wqk_b", [128, 2, MD * 128], BF16))
    wvu_b = e1.enter_context(sb("wvu_b", [128, 2, MD * 256], BF16))
    qk_st = e1.enter_context(sb("qk_st", [128, 2, T], BF16))
    vu_st = e1.enter_context(sb("vu_st", [128, 2, 256], BF16))

    ps4 = [e1.enter_context(nc.psum_tensor(f"ps_e1_{i}", [128, 512], F32)) for i in range(4)]
    ps_s1 = e1.enter_context(nc.psum_tensor("ps_s1", [1, 1], F32))
    ps_s128 = e1.enter_context(nc.psum_tensor("ps_s128", [128, 1], F32))
    ps_xt = [e1.enter_context(nc.psum_tensor(f"ps_xt{i}", [128, 512], BF16)) for i in range(2)]

    t_xload, t_ln, t_sq, t_xn, t_xn_spill = {}, {}, {}, {}, {}
    t_xnT_tr, t_xnT_ev = {}, {}

    def emit_ln1_apply(m):
        o = (m % 2) * 8

        @blk.vector
        def _(v, m=m, o=o):
            v.wait_ge(am.h, t_sq[m])
            if m >= 2:
                v.wait_ge(tm.h, t_xnT_tr[(m - 2, 3)])
            t_xn[m] = t_inc(v.tensor_scalar(out=xn1_bf[:, m % 2, :], in0=x_buf[:, m % 2, :], scalar1=stat[:, o:o + 1], scalar2=stat[:, o + 4:o + 5], op0=ALU.subtract, op1=ALU.mult), vm)

        for g in range(4):
            gi = m * 4 + g
            stg = gi % 2

            @blk.tensor
            def _(t, m=m, g=g, gi=gi, stg=stg):
                if g == 0:
                    t.wait_ge(vm.h, t_xn[m])
                prev = t_xnT_ev.get(gi - 2, 0)
                if prev:
                    t.wait_ge(vm.h, prev)
                for i in range(4):
                    K = g * 4 + i
                    mmi = t.transpose(ps_xt[stg][:, i * 128:(i + 1) * 128], xn1_bf[:, m % 2, K * 128:(K + 1) * 128], ident[:, :])
                t_xnT_tr[(m, g)] = t_inc(mmi, tm)

            @blk.vector
            def _(v, m=m, g=g, gi=gi, stg=stg):
                v.wait_ge(tm.h, t_xnT_tr[(m, g)])
                t_xnT_ev[gi] = t_inc(
                    v.tensor_copy(out=xn1T[:, :].rearrange("p (kc t2) -> p kc t2", t2=T)[:, g * 4:(g + 1) * 4, m * 128:(m + 1) * 128],
                                  in_=ps_xt[stg][:, :].rearrange("p (i t2) -> p i t2", t2=128)), vm)

    for m in range(MT):
        o = (m % 2) * 8

        @blk.sync
        def _(sy, m=m):
            if m >= 2:
                sy.wait_ge(vm.h, t_xn[m - 2])
            t_xload[m] = dma16(sy, SXP, out=x_buf[:, m % 2, :], in_=x_in[m * 128:(m + 1) * 128, :])

        @blk.vector
        def _(v, m=m, o=o):
            wt(v, t_xload[m])
            xb = x_buf[:, m % 2, :]
            v.tensor_scalar(out=dump[:, :], in0=xb, scalar1=1.0, scalar2=0.0, op0=ALU.mult, op1=ALU.add, accum_out=stat[:, o:o + 1])
            v.drain()
            v.tensor_scalar_mul(out=stat[:, o:o + 1], in0=stat[:, o:o + 1], scalar1=1.0 / D)
            v.drain()
            v.scalar_tensor_tensor(out=dump[:, :], in0=xb, scalar=stat[:, o:o + 1], in1=xb, op0=ALU.subtract, op1=ALU.mult, accum_out=stat[:, o + 1:o + 2])
            v.drain()
            t_ln[m] = t_inc(v.tensor_scalar(out=stat[:, o + 2:o + 3], in0=stat[:, o + 1:o + 2], scalar1=1.0 / D, scalar2=EPS_LN, op0=ALU.mult, op1=ALU.add), vm)

        def f_sq(s, m=m, o=o):
            s.wait_ge(vm.h, t_ln[m])
            s.activation(out=stat[:, o + 3:o + 4], in_=stat[:, o + 2:o + 3], func=AF.Ln)
            s.drain()
            s.activation(out=stat[:, o + 4:o + 5], in_=stat[:, o + 3:o + 4], func=AF.Exp, scale=epsc[:, 2:3]).then_inc(am.h, 1)
        t_sq[m] = sadd(f_sq)

        if m >= 1:
            emit_ln1_apply(m - 1)
    emit_ln1_apply(MT - 1)
    t_xn1T_v = t_xnT_ev[MT * 4 - 1]

    KSTAGE = os.environ.get("KSTAGE", "full")
    w1p = weight_prep_make(e1, "w1", w1T, D_w1s, D, DFF, 4096, 0)
    DO_QK = KSTAGE in ("qk", "vu", "full")
    DO_VU = KSTAGE in ("vu", "full")
    DO_WP = (KSTAGE == "full")
    t_wqk, t_qk_mm, t_qk_ev, t_qk_sp = {}, {}, {}, {}

    @blk.gpsimd
    def _(g):
        if not DO_QK:
            return
        for Mc in range(2):
            t_wqk[Mc] = dma16(g, GXP, out=wqk_b[:, Mc % 2, :].rearrange("p (kc mm) -> p kc mm", mm=128),
                                          in_=wqkT[:, Mc * 128:(Mc + 1) * 128].rearrange("(kc p) mm -> p kc mm", p=128))
    if DO_WP:
        for _pf in range(w1p["nslot"]):
            w1p["emit_dma"](_pf)

    for Mc in range(2 * MD if DO_QK else 0):
        @blk.tensor
        def _(t, Mc=Mc):
            if Mc == 0:
                t.wait_ge(vm.h, t_xn1T_v)
            wt(t, t_wqk[Mc])
            for n in range(4):
                if Mc >= 1:
                    t.wait_ge(am.h, t_qk_ev[(Mc - 1, n)])
                for K in range(MD):
                    mmi = t.matmul(ps4[n][:, :], wqk_b[:, Mc % 2, K * 128:(K + 1) * 128],
                                   xn1T[:, K * T + n * 512:K * T + (n + 1) * 512],
                                   start=(K == 0), stop=(K == MD - 1))
                t_qk_mm[(Mc, n)] = t_inc(mmi, tm)

        toks_mm = dict(t_qk_mm)
        tsp_qk = t_qk_sp.get(Mc - 2)

        def f_ev(s, Mc=Mc, toks_mm=toks_mm, tsp_qk=tsp_qk):
            for n in range(4):
                s.wait_ge(tm.h, toks_mm[(Mc, n)])
                if tsp_qk is not None and n == 0:
                    wt(s, tsp_qk)
                s.activation(out=qk_st[:, Mc % 2, n * 512:(n + 1) * 512], in_=ps4[n][:, :], func=AF.Silu).then_inc(am.h, 1)
        sadd(f_ev, ninc=4)
        for n in range(4):
            t_qk_ev[(Mc, n)] = am.n - 3 + n

        @blk.sync
        def _(sy, Mc=Mc):
            sy.wait_ge(am.h, t_qk_ev[(Mc, 3)])
            dst = D_qfm if Mc < MD else D_kfm
            r = (Mc % MD) * 128
            t_qk_sp[Mc] = dma16(sy, SXP, out=dst[r:r + 128, :], in_=qk_st[:, Mc % 2, :])

        if Mc + 2 < 2 * MD:
            @blk.gpsimd
            def _(g, Mc=Mc):
                g.wait_ge(tm.h, t_qk_mm[(Mc, 3)])
                t_wqk[Mc + 2] = dma16(g, GXP, out=wqk_b[:, Mc % 2, :].rearrange("p (kc mm) -> p kc mm", mm=128),
                                                  in_=wqkT[:, (Mc + 2) * 128:(Mc + 3) * 128].rearrange("(kc p) mm -> p kc mm", p=128))
        if DO_WP:
            w1p["emit_chunk"](Mc)

    NVU = 16
    t_wvu, t_vu_mm, t_vu_ev, t_vu_sp = {}, {}, {}, {}

    @blk.gpsimd
    def _(g):
        if not DO_VU:
            return
        for n in range(2):
            t_wvu[n] = dma16(g, GXP, out=wvu_b[:, n % 2, :].rearrange("p (kc j) -> p kc j", j=256),
                                         in_=wvuT[:, n * 256:(n + 1) * 256].rearrange("(kc p) j -> p kc j", p=128))

    for n in range(NVU if DO_VU else 0):
        for m in range(MT):
            @blk.tensor
            def _(t, n=n, m=m):
                if m == 0:
                    wt(t, t_wvu[n])
                idx = n * MT + m
                if idx >= 4:
                    pidx = idx - 4
                    t.wait_ge(vm.h, t_vu_ev[(pidx // MT, pidx % MT)])
                elif idx < 4:
                    t.wait_ge(am.h, t_qk_ev[(2 * MD - 1, m % 4)])
                for K in range(MD):
                    mmi = t.matmul(ps4[m % 4][:, 0:256], xn1T[:, K * T + m * 128:K * T + m * 128 + 128],
                                   wvu_b[:, n % 2, K * 256:(K + 1) * 256],
                                   start=(K == 0), stop=(K == MD - 1))
                t_vu_mm[(n, m)] = t_inc(mmi, tm)

            @blk.vector
            def _(v, n=n, m=m):
                v.wait_ge(tm.h, t_vu_mm[(n, m)])
                idx = n * MT + m
                if idx >= 2:
                    pidx = idx - 2
                    wt(v, t_vu_sp[(pidx // MT, pidx % MT)])
                t_vu_ev[(n, m)] = t_inc(v.tensor_copy(out=vu_st[:, idx % 2, :], in_=ps4[m % 4][:, 0:256]), vm)

            @blk.sync
            def _(sy, n=n, m=m):
                sy.wait_ge(vm.h, t_vu_ev[(n, m)])
                idx = n * MT + m
                if n < 8:
                    dst = D_v[m * 128:(m + 1) * 128, n * 256:(n + 1) * 256]
                else:
                    dst = D_u[m * 128:(m + 1) * 128, (n - 8) * 256:(n - 8 + 1) * 256]
                t_vu_sp[(n, m)] = dma16(sy, SXP, out=dst, in_=vu_st[:, idx % 2, :])

        if n + 2 < NVU:
            @blk.gpsimd
            def _(g, n=n):
                g.wait_ge(tm.h, t_vu_mm[(n, MT - 1)])
                t_wvu[n + 2] = dma16(g, GXP, out=wvu_b[:, n % 2, :].rearrange("p (kc j) -> p kc j", j=256),
                                                 in_=wvuT[:, (n + 2) * 256:(n + 3) * 256].rearrange("(kc p) j -> p kc j", p=128))

    if DO_WP:
        finish_scale(D * DFF, 0, w1p["abs_last"](), w1p["nchunk"], 0, ps_s1[:, :], ps_s128[:, :], stat[:, 5:6])

    barrier()
    if upto <= 1:
        e1.close(); eA.close(); ctx.close()
        return nc
    e1.close()

    e2 = contextlib.ExitStack()
    qT_b = e2.enter_context(sb("qT_b", [128, 2, H * BLK], BF16))
    kT_b = e2.enter_context(sb("kT_b", [128, 2, H * BLK], BF16))
    v_b = e2.enter_context(sb("v_b", [128, 2, 2 * D], BF16))
    u_b = e2.enter_context(sb("u_b", [128, 2, 2 * D], BF16))
    ktm = e2.enter_context(sb("ktm", [128, H * BLK], BF16))
    vp = e2.enter_context(sb("vp", [128, 2 * D], BF16))
    kqs = e2.enter_context(sb("kqs", [128, 2, 2 * BLK], BF16))
    kv = e2.enter_context(sb("kv", [128, H * HD], F32))
    kvb = e2.enter_context(sb("kvb", [128, H * HD], BF16))
    gath = e2.enter_context(sb("gath", [128, 4, H * HD], BF16))
    o_blk = e2.enter_context(sb("o_blk", [128, 2, D], F32))
    qTd = e2.enter_context(sb("qTd", [128, 3, BLK], BF16))
    z_bf = e2.enter_context(sb("z_bf", [128, 2, D], BF16))
    z_ts = e2.enter_context(sb("z_ts", [128, 2, 512], BF16))
    rms_c = e2.enter_context(sb("rms_c", [128, 16], F32))

    ps_tr = [e2.enter_context(nc.psum_tensor(f"ps_tr{i}", [128, 512], BF16)) for i in range(2)]
    ps_kq = [e2.enter_context(nc.psum_tensor(f"ps_kq{i}", [128, 512], F32)) for i in range(2)]
    ps_nddg = [e2.enter_context(nc.psum_tensor(f"ps_nddg{i}", [128, 512], F32)) for i in range(3)]
    ps_kvu = [e2.enter_context(nc.psum_tensor(f"ps_kvu{i}", [128, HD], F32)) for i in range(1)]

    def stream_block(sy, j, need_qu, tokens, extra_wait=None):
        if extra_wait is not None:
            sem_, val_ = extra_wait
            sy.wait_ge(sem_, val_)
        sl = slice(j * BLK, (j + 1) * BLK)
        dma16(sy, SXP, out=kT_b[:, j % 2, :].rearrange("p (hh t) -> p hh t", t=BLK),
                           in_=D_kfm[:, sl].rearrange("(hh p) t -> p hh t", p=128))
        dma16(sy, SXP, out=v_b[:, j % 2, :].rearrange("p (c f) -> p c f", f=D),
                           in_=D_v[sl, :].rearrange("(c p) f -> p c f", p=128))
        if need_qu:
            dma16(sy, SXP, out=qT_b[:, j % 2, :].rearrange("p (hh t) -> p hh t", t=BLK),
                               in_=D_qfm[:, sl].rearrange("(hh p) t -> p hh t", p=128))
            dma16(sy, SXP, out=u_b[:, j % 2, :].rearrange("p (c f) -> p c f", f=D),
                               in_=D_u[sl, :].rearrange("(c p) f -> p c f", p=128))
        tokens[j] = snap(SXP)

    w2p = weight_prep_make(e2, "w2", w2T, D_w2s, D, DFF, 4096, 32, nslot=2)
    for _pf in range(w2p["nslot"]):
        w2p["emit_dma"](_pf)

    t_a_ld, t_a_tr, t_a_ktm, t_a_vp, t_a_mm, t_a_kv = {}, {}, {}, {}, {}, {}

    @blk.vector
    def _(v):
        v.wait_ge(am.h, prep_done)
        t_inc(v.memset(kv[:, :], 0.0), vm)
    t_kv_zero = vm.n

    @blk.sync
    def _(sy):
        for j in range(2):
            stream_block(sy, j, False, t_a_ld)

    for j in range(NBLK):
        for grp in range(8):
            st = grp % 2
            if grp % 4 == 0:
                w2p["emit_chunk"](j * 2 + grp // 4)

            @blk.tensor
            def _(t, j=j, grp=grp, st=st):
                if grp == 0:
                    wt(t, t_a_ld[j])
                prev = t_a_ktm.get((j, grp - 2), t_a_ktm.get((j - 1, 6 + st), 0))
                if prev:
                    t.wait_ge(vm.h, prev)
                for i in range(4):
                    tile = grp * 4 + i
                    h, c = tile // 2, tile % 2
                    mmi = t.transpose(ps_tr[st][:, i * 128:(i + 1) * 128], kT_b[:, j % 2, h * BLK + c * 128:h * BLK + (c + 1) * 128], ident[:, :])
                t_a_tr[(j, grp)] = t_inc(mmi, tm)

            @blk.vector
            def _(v, j=j, grp=grp):
                v.wait_ge(tm.h, t_a_tr[(j, grp)])
                if j > 0:
                    v.wait_ge(tm.h, t_a_mm[(j - 1, 2 * grp + 1)])
                t_a_ktm[(j, grp)] = t_inc(v.tensor_copy(out=ktm[:, grp * 512:(grp + 1) * 512], in_=ps_tr[grp % 2][:, :]), vm)

        @blk.vector
        def _(v, j=j):
            wt(v, t_a_ld[j])
            if j > 0:
                v.wait_ge(tm.h, t_a_mm[(j - 1, H - 1)])
            for c in range(2):
                v.tensor_tensor(out=vp[:, c * D:(c + 1) * D].rearrange("p (hh e) -> p hh e", e=128),
                                in0=v_b[:, j % 2, c * D:(c + 1) * D].rearrange("p (hh e) -> p hh e", e=128),
                                in1=kdec[:, c, :].unsqueeze(2).broadcast_to([128, H, 128]), op=ALU.mult)
            t_a_vp[j] = t_inc(v.engine_nop(), vm)

        for h in range(H):
            st = h % 2

            @blk.tensor
            def _(t, j=j, h=h, st=st):
                if h == 0:
                    t.wait_ge(vm.h, t_a_vp[j])
                prev = t_a_kv.get((j, h - 1), t_a_kv.get((j - 1, H - 1), 0))
                if prev:
                    t.wait_ge(vm.h, prev)
                t.wait_ge(vm.h, t_a_ktm[(j, h // 2)])
                for c in range(2):
                    mmi = t.matmul(ps_kvu[0][:, :], ktm[:, h * BLK + c * 128:h * BLK + (c + 1) * 128],
                                   vp[:, c * D + h * 128:c * D + h * 128 + 128], start=(c == 0), stop=(c == 1))
                t_a_mm[(j, h)] = t_inc(mmi, tm)

            @blk.vector
            def _(v, j=j, h=h, st=st):
                v.wait_ge(tm.h, t_a_mm[(j, h)])
                t_a_kv[(j, h)] = t_inc(
                    v.scalar_tensor_tensor(out=kv[:, h * 128:(h + 1) * 128], in0=kv[:, h * 128:(h + 1) * 128],
                                           scalar=bdec[:, h:h + 1], in1=ps_kvu[0][:, :], op0=ALU.mult, op1=ALU.add), vm)

        if j + 2 < NBLK:
            @blk.sync
            def _(sy, j=j):
                stream_block(sy, j + 2, False, t_a_ld, extra_wait=(tm.h, t_a_mm[(j, H - 1)]))
    t_kvf_done = t_a_kv[(NBLK - 1, H - 1)]

    if os.environ.get("E2STAGE", "full") == "2a0":
        @blk.sync
        def _(sy):
            sy.wait_ge(vm.h, t_kvf_done)
            dma16(sy, SXP, out=D_kv0[:, :], in_=kv[:, :])
        barrier()
        e2.close(); eA.close(); ctx.close()
        return nc

    cc_toks = {}

    @blk.gpsimd
    def _(g):
        g.wait_ge(vm.h, t_kvf_done)
        cc_toks["cci"] = dma16(g, GXP, out=D_cci.ap().rearrange("(hh dd) e -> dd hh e", dd=128),
                               in_=kv[:, :].rearrange("p (hh e) -> p hh e", e=128))
        wt(g, cc_toks["cci"])
        t_inc(g.collective_compute("AllGather", ALU.bypass, replica_groups=[list(range(NC))],
                                   ins=[D_cci.ap().opt()], outs=[D_cco.ap().opt()]), cc, 1)
        g.wait_ge(cc.h, cc.n)

    t_gath, t_sel = {}, {}

    for r in range(NC):
        @blk.gpsimd
        def _(g, r=r):
            if r >= 4:
                g.wait_ge(vm.h, t_sel[r - 4])
            t_gath[r] = dma16(g, GXP, out=gath[:, r % 4, :].rearrange("p (hh e) -> p hh e", e=128),
                                          in_=D_cco.ap()[r * 2048:(r + 1) * 2048, :].rearrange("(hh dd) e -> dd hh e", dd=128))

        @blk.vector
        def _(v, r=r):
            if r == 0:
                v.memset(kvb[:, :], 0.0)
                v.drain()
            wt(v, t_gath[r])
            v.drain()
            t_sel[r] = t_inc(v.scalar_tensor_tensor(out=kvb[:, :], in0=gath[:, r % 4, :], scalar=kvmask_s[:, r:r + 1],
                                                    in1=kvb[:, :], op0=ALU.mult, op1=ALU.add), vm)
    t_kv0 = t_sel[NC - 1]

    @blk.sync
    def _(sy):
        sy.wait_ge(vm.h, t_kv0)
        dma16(sy, SXP, out=D_kv0[:, :], in_=kv[:, :])

    E2STAGE = os.environ.get("E2STAGE", "full")
    if E2STAGE == "2a":
        barrier()
        e2.close(); eA.close(); ctx.close()
        return nc

    t_s_ld, t_s_tr, t_s_ktm, t_s_vp = {}, {}, {}, {}
    t_s_kq, t_s_kqs, t_s_nddg, t_s_kvu, t_s_kvn, t_s_qtd, t_s_oev = {}, {}, {}, {}, {}, {}, {}
    t_s_z, t_s_zsp, t_s_ztr, t_s_zte = {}, {}, {}, {}

    @blk.sync
    def _(sy):
        sy.wait_ge(tm.h, t_a_tr[(NBLK - 2, 7)])
        sy.wait_ge(vm.h, t_a_vp[NBLK - 2])
        stream_block(sy, 0, True, t_s_ld)
        sy.wait_ge(tm.h, t_a_mm[(NBLK - 1, H - 1)])
        stream_block(sy, 1, True, t_s_ld)

    for j in range(NBLK):
        for grp in range(8):
            st = grp % 2

            @blk.tensor
            def _(t, j=j, grp=grp, st=st):
                if j == 0 and grp == 0:
                    t.wait_ge(vm.h, t_kv0)
                if grp == 0:
                    wt(t, t_s_ld[j])
                prev = t_s_ktm.get((j, grp - 2), t_s_ktm.get((j - 1, 6 + st), t_a_ktm[(NBLK - 1, 6 + st)]))
                t.wait_ge(vm.h, prev)
                if grp < 2 and (j - 1, 6 + st) in t_s_zte:
                    t.wait_ge(vm.h, t_s_zte[(j - 1, 6 + st)])
                for i in range(4):
                    tile = grp * 4 + i
                    h, c = tile // 2, tile % 2
                    mmi = t.transpose(ps_tr[st][:, i * 128:(i + 1) * 128], kT_b[:, j % 2, h * BLK + c * 128:h * BLK + (c + 1) * 128], ident[:, :])
                t_s_tr[(j, grp)] = t_inc(mmi, tm)

            @blk.vector
            def _(v, j=j, grp=grp):
                v.wait_ge(tm.h, t_s_tr[(j, grp)])
                kprev = t_s_kvu.get((j - 1, 2 * grp + 1), t_a_mm[(NBLK - 1, 2 * grp + 1)])
                v.wait_ge(tm.h, kprev)
                t_s_ktm[(j, grp)] = t_inc(v.tensor_copy(out=ktm[:, grp * 512:(grp + 1) * 512], in_=ps_tr[grp % 2][:, :]), vm)

        @blk.vector
        def _(v, j=j):
            wt(v, t_s_ld[j])
            vprev = t_s_kvu.get((j - 1, H - 1), t_a_mm[(NBLK - 1, H - 1)])
            v.wait_ge(tm.h, vprev)
            for c in range(2):
                v.tensor_tensor(out=vp[:, c * D:(c + 1) * D].rearrange("p (hh e) -> p hh e", e=128),
                                in0=v_b[:, j % 2, c * D:(c + 1) * D].rearrange("p (hh e) -> p hh e", e=128),
                                in1=kdec[:, c, :].unsqueeze(2).broadcast_to([128, H, 128]), op=ALU.mult)
            pq0 = t_s_nddg.get((j - 1, H - 3), 0)
            if pq0:
                v.wait_ge(tm.h, pq0)
            t_s_qtd[(j, 0)] = t_inc(v.tensor_tensor(out=qTd[:, 0, :], in0=qT_b[:, j % 2, 0:BLK],
                                                    in1=qdecT[:, 0, :], op=ALU.mult), vm)
            t_s_vp[j] = t_inc(v.engine_nop(), vm)

        def emit_diag(h2, j=j):
            st2 = h2 % 2
            sd32 = h2 % 3

            @blk.tensor
            def _(t, j=j, h2=h2, st2=st2, sd32=sd32):
                if h2 == 0:
                    t.wait_ge(vm.h, t_s_vp[j])
                po = t_s_oev.get((j, h2 - 3), t_s_oev.get((j - 1, H + h2 - 3), 0))
                if po:
                    t.wait_ge(am.h, po)
                pn = t_s_kvn.get((j - 1, h2), t_kv0)
                t.wait_ge(vm.h, pn)
                t.wait_ge(vm.h, t_s_qtd[(j, h2)])
                t.wait_ge(vm.h, t_s_kqs[(j, h2)])
                t.matmul(ps_nddg[sd32][:, 256:384], qTd[:, sd32, 0:128], kvb[:, h2 * 128:(h2 + 1) * 128], start=True, stop=False)
                t.matmul(ps_nddg[sd32][:, 256:384], kqs[:, st2, 0:128], v_b[:, j % 2, h2 * 128:h2 * 128 + 128], start=False, stop=True)
                t.matmul(ps_nddg[sd32][:, 384:512], qTd[:, sd32, 128:256], kvb[:, h2 * 128:(h2 + 1) * 128], start=True, stop=False)
                t.matmul(ps_nddg[sd32][:, 384:512], kqs[:, st2, 128:256], v_b[:, j % 2, h2 * 128:h2 * 128 + 128], start=False, stop=False)
                mmi = t.matmul(ps_nddg[sd32][:, 384:512], kqs[:, st2, BLK + 128:BLK + 256], v_b[:, j % 2, D + h2 * 128:D + h2 * 128 + 128], start=False, stop=True)
                t_s_nddg[(j, h2)] = t_inc(mmi, tm)
                pkv = t_s_kvn.get((j, h2 - 1), t_s_kvn.get((j - 1, H - 1), 0))
                if pkv:
                    t.wait_ge(vm.h, pkv)
                t.wait_ge(vm.h, t_s_ktm[(j, h2 // 2)])
                for c in range(2):
                    mmi = t.matmul(ps_kvu[0][:, :], ktm[:, h2 * BLK + c * 128:h2 * BLK + (c + 1) * 128],
                                   vp[:, c * D + h2 * 128:c * D + h2 * 128 + 128], start=(c == 0), stop=(c == 1))
                t_s_kvu[(j, h2)] = t_inc(mmi, tm)

        def emit_o(h2, j=j):
            tnd = t_s_nddg[(j, h2)]
            sd32 = h2 % 3

            def f_oev(s, j=j, h2=h2, sd32=sd32, tnd=tnd):
                s.wait_ge(tm.h, tnd)
                if j > 0 and h2 <= 1:
                    wt(s, t_s_zsp[j - 1])
                for cq in range(2):
                    mmi = s.activation(out=o_blk[:, cq, h2 * 128:(h2 + 1) * 128],
                                       in_=ps_nddg[sd32][:, 256 + cq * 128:256 + (cq + 1) * 128], func=AF.Copy)
                mmi.then_inc(am.h, 1)
            t_s_oev[(j, h2)] = sadd(f_oev)

        for h in range(H):
            st = h % 2

            @blk.tensor
            def _(t, j=j, h=h, st=st):
                pk = t_s_kqs.get((j, h - 2), t_s_kqs.get((j - 1, H - 2 + st), 0))
                if pk:
                    t.wait_ge(vm.h, pk)
                for kc in range(2):
                    mmi = t.matmul(ps_kq[st][:, kc * BLK:(kc + 1) * BLK],
                                   kT_b[:, j % 2, h * BLK + kc * 128:h * BLK + (kc + 1) * 128],
                                   qT_b[:, j % 2, h * BLK:(h + 1) * BLK], start=True, stop=True)
                t_s_kq[(j, h)] = t_inc(mmi, tm)

            if h >= 1:
                emit_diag(h - 1)

            @blk.vector
            def _(v, j=j, h=h, st=st):
                v.wait_ge(tm.h, t_s_kq[(j, h)])
                if h + 1 < H:
                    t_s_qtd[(j, h + 1)] = t_inc(
                        v.tensor_tensor(out=qTd[:, (h + 1) % 3, :], in0=qT_b[:, j % 2, (h + 1) * BLK:(h + 2) * BLK],
                                        in1=qdecT[:, h + 1, :], op=ALU.mult), vm)
                pd = t_s_nddg.get((j, h - 2), t_s_nddg.get((j - 1, H - 2 + st), 0))
                if pd:
                    v.wait_ge(tm.h, pd)
                v.tensor_tensor(out=kqs[:, st, :], in0=ps_kq[st][:, :],
                                in1=decayT[:, h, :, :].rearrange("p a b -> p (a b)"), op=ALU.mult)
                t_s_kqs[(j, h)] = t_inc(v.engine_nop(), vm)
                if h >= 1:
                    v.wait_ge(tm.h, t_s_kvu[(j, h - 1)])
                    t_s_kvn[(j, h - 1)] = t_inc(
                        v.scalar_tensor_tensor(out=kvb[:, (h - 1) * 128:h * 128], in0=kvb[:, (h - 1) * 128:h * 128],
                                               scalar=bdec[:, h - 1:h], in1=ps_kvu[0][:, :], op0=ALU.mult, op1=ALU.add), vm)

            if h >= 1:
                emit_o(h - 1)
            if h % 8 == 0:
                w2p["emit_chunk"](16 + j * 2 + h // 8)
        emit_diag(H - 1)

        @blk.vector
        def _(v, j=j):
            v.wait_ge(tm.h, t_s_kvu[(j, H - 1)])
            t_s_kvn[(j, H - 1)] = t_inc(
                v.scalar_tensor_tensor(out=kvb[:, (H - 1) * 128:H * 128], in0=kvb[:, (H - 1) * 128:H * 128],
                                       scalar=bdec[:, H - 1:H], in1=ps_kvu[0][:, :], op0=ALU.mult, op1=ALU.add), vm)

        emit_o(H - 1)

        for cq in range(2):
            o = cq * 6

            @blk.vector
            def _(v, j=j, cq=cq, o=o):
                if cq == 0:
                    v.wait_ge(am.h, t_s_oev[(j, H - 1)])
                v.drain()
                v.scalar_tensor_tensor(out=gath[:, 0, :], in0=o_blk[:, cq, :], scalar=0.0, in1=o_blk[:, cq, :],
                                       op0=ALU.subtract, op1=ALU.mult, accum_out=rms_c[:, o:o + 1])
                v.drain()
                t_inc(v.tensor_scalar(out=rms_c[:, o + 1:o + 2], in0=rms_c[:, o:o + 1], scalar1=1.0 / D, scalar2=EPS_RMS, op0=ALU.mult, op1=ALU.add), vm)
            t_rms1 = vm.n

            def f_rms(s, o=o, t_rms1=t_rms1):
                s.wait_ge(vm.h, t_rms1)
                s.activation(out=rms_c[:, o + 2:o + 3], in_=rms_c[:, o + 1:o + 2], func=AF.Ln)
                s.drain()
                s.activation(out=rms_c[:, o + 3:o + 4], in_=rms_c[:, o + 2:o + 3], func=AF.Exp, scale=epsc[:, 2:3]).then_inc(am.h, 1)
            t_rms2 = sadd(f_rms)

            @blk.vector
            def _(v, j=j, cq=cq, o=o, t_rms2=t_rms2):
                v.wait_ge(am.h, t_rms2)
                v.drain()
                if j > 0 and cq == 0:
                    v.wait_ge(tm.h, t_s_ztr[(j - 1, 7)])
                t_s_z[(j, cq)] = t_inc(
                    v.scalar_tensor_tensor(out=z_bf[:, cq, :], in0=o_blk[:, cq, :], scalar=rms_c[:, o + 3:o + 4],
                                           in1=u_b[:, j % 2, cq * D:(cq + 1) * D], op0=ALU.mult, op1=ALU.mult), vm)
        t_s_z[j] = t_s_z[(j, 1)]

        for c2 in range(2):
            for g2 in range(4):
                gi2 = c2 * 4 + g2

                @blk.tensor
                def _(t, j=j, c2=c2, g2=g2, gi2=gi2):
                    st2_ = gi2 % 2
                    if gi2 == 0:
                        t.wait_ge(vm.h, t_s_z[j])
                    prev = t_s_zte.get((j, gi2 - 2), t_s_ktm[(j, 6 + st2_)])
                    t.wait_ge(vm.h, prev)
                    for i in range(4):
                        K = g2 * 4 + i
                        mmi = t.transpose(ps_tr[st2_][:, i * 128:(i + 1) * 128], z_bf[:, c2, K * 128:(K + 1) * 128], ident[:, :])
                    t_s_ztr[(j, gi2)] = t_inc(mmi, tm)

                @blk.vector
                def _(v, j=j, c2=c2, g2=g2, gi2=gi2):
                    st2_ = gi2 % 2
                    v.wait_ge(tm.h, t_s_ztr[(j, gi2)])
                    tokp = t_s_zsp.get((j, gi2 - 2), t_s_zsp.get((j - 1, 6 + st2_)))
                    if tokp is not None:
                        wt(v, tokp)
                    t_s_zte[(j, gi2)] = t_inc(
                        v.tensor_copy(out=z_ts[:, gi2 % 2, :], in_=ps_tr[st2_][:, :]), vm)

                @blk.sync
                def _(sy, j=j, c2=c2, g2=g2, gi2=gi2):
                    sy.wait_ge(vm.h, t_s_zte[(j, gi2)])
                    t_s_zsp[(j, gi2)] = dma16(sy, SXP,
                        out=D_zT[g2 * 512:(g2 + 1) * 512, j * BLK + c2 * 128:j * BLK + (c2 + 1) * 128].rearrange("(i p) t2 -> p i t2", p=128),
                        in_=z_ts[:, gi2 % 2, :].rearrange("p (i t2) -> p i t2", t2=128))
        t_s_zsp[j] = t_s_zsp[(j, 7)]

        if j + 2 < NBLK:
            @blk.sync
            def _(sy, j=j):
                sy.wait_ge(tm.h, t_s_nddg[(j, H - 1)])
                sy.wait_ge(vm.h, t_s_z[j])
                stream_block(sy, j + 2, True, t_s_ld)

    finish_scale(D * DFF, 1, w2p["abs_last"](), w2p["nchunk"], 32,
                 ps_kq[0][0:1, 0:1], ps_kq[1][:, 0:1], rms_c[:, 15:16],
                 pe_wait_vm=t_s_kvn[(NBLK - 1, H - 1)])

    barrier()
    if upto <= 2:
        e2.close(); eA.close(); ctx.close()
        return nc
    e2.close()
    eA.close()

    e2c = contextlib.ExitStack()
    wo_b = e2c.enter_context(sb("wo_b", [128, MD * D], BF16))
    zT_b = e2c.enter_context(sb("zT_b", [128, 2, MD * 512], BF16))
    x_b2 = e2c.enter_context(sb("x_b2", [128, 2, D], F32))
    at_sb = e2c.enter_context(sb("at_sb", [128, 2, D], F32))
    x2_sb = e2c.enter_context(sb("x2_sb", [128, 2, D], F32))
    y_sb = e2c.enter_context(sb("y_sb", [128, 2, D], F32))
    qy_bf = e2c.enter_context(sb("qy_bf", [128, 2, D], BF16))
    st2 = e2c.enter_context(sb("st2", [128, 48], F32))
    gq = e2c.enter_context(sb("gq", [128, 96], F32))
    dump3 = e2c.enter_context(sb("dump3", [128, 2, D], F32))
    dumpA = e2c.enter_context(sb("dumpA", [128, 2, D], F32))
    gq2 = e2c.enter_context(sb("gq2", [128, 2, 96], F32))

    ps4c = [e2c.enter_context(nc.psum_tensor(f"ps_e2c_{i}", [128, 512], F32)) for i in range(4)]
    ps_qyt = [e2c.enter_context(nc.psum_tensor(f"ps_qyt{i}", [128, 512], BF16)) for i in range(2)]
    qyt_st = e2c.enter_context(sb("qyt_st", [128, 2, 512], BF16))

    t_wo = {}

    @blk.gpsimd
    def _(g):
        for q in range(4):
            t_wo[q] = dma16(g, GXP, out=wo_b[:, :].rearrange("p (kc n2) -> p kc n2", n2=D)[:, 4 * q:4 * (q + 1), :],
                                        in_=woT[:, :].rearrange("(kc p) n2 -> p kc n2", p=128)[:, 4 * q:4 * (q + 1), :])
        t_wo["all"] = snap(GXP)

    t_zT, t_x2ld = {}, {}
    t_wo_mm, t_wo_ev, t_x2c = {}, {}, {}
    t_ln2, t_ln2b, t_ln3s, t_ln3a, t_ln3b = {}, {}, {}, {}, {}
    t_qy, t_qysp, t_x2sp = {}, {}, {}
    t_qyt_tr, t_qyt_ev, t_qyt_sp = {}, {}, {}

    def emit_qyT(m2):
        for g2 in range(4):
            gi2 = m2 * 4 + g2

            @blk.tensor
            def _(t, m2=m2, g2=g2, gi2=gi2):
                if g2 == 0:
                    t.wait_ge(vm.h, t_qy[m2])
                prev = t_qyt_ev.get(gi2 - 2, 0)
                if prev:
                    t.wait_ge(vm.h, prev)
                for i in range(4):
                    K = g2 * 4 + i
                    mmi = t.transpose(ps_qyt[gi2 % 2][:, i * 128:(i + 1) * 128], qy_bf[:, m2 % 2, K * 128:(K + 1) * 128], ident[:, :])
                t_qyt_tr[(m2, g2)] = t_inc(mmi, tm)

            @blk.vector
            def _(v, m2=m2, g2=g2, gi2=gi2):
                v.wait_ge(tm.h, t_qyt_tr[(m2, g2)])
                tokp = t_qyt_sp.get(gi2 - 2)
                if tokp is not None:
                    wt(v, tokp)
                t_qyt_ev[gi2] = t_inc(v.tensor_copy(out=qyt_st[:, gi2 % 2, :], in_=ps_qyt[gi2 % 2][:, :]), vm)

            @blk.sync
            def _(sy, m2=m2, g2=g2, gi2=gi2):
                sy.wait_ge(vm.h, t_qyt_ev[gi2])
                t_qyt_sp[gi2] = dma16(sy, SXP,
                    out=D_qayT[g2 * 512:(g2 + 1) * 512, m2 * 128:(m2 + 1) * 128].rearrange("(i p) t2 -> p i t2", p=128),
                    in_=qyt_st[:, gi2 % 2, :].rearrange("p (i t2) -> p i t2", t2=128))

    def s_mm(m):
        gi, mi = m // 4, m % 4
        if mi == 0:
            @blk.gpsimd
            def _(g_, gi=gi):
                if gi >= 2:
                    g_.wait_ge(tm.h, t_wo_mm[(gi - 2, 3, 3)])
                t_zT[gi] = dma16(g_, GXP,
                    out=zT_b[:, gi % 2, :].rearrange("p (kc t2) -> p kc t2", t2=512),
                    in_=D_zT[:, gi * 512:(gi + 1) * 512].rearrange("(kc p) t2 -> p kc t2", p=128))

        @blk.sync
        def _(sy, m=m):
            if m >= 2:
                sy.wait_ge(vm.h, t_x2c.get(m - 2, 0))
            t_x2ld[m] = dma16(sy, SXP, out=x_b2[:, m % 2, :], in_=x_in[m * 128:(m + 1) * 128, :])

        @blk.tensor
        def _(t, gi=gi, mi=mi, m=m):
            if mi == 0:
                wt(t, t_zT[gi])
                if gi == 0:
                    wt(t, t_wo["all"])
            for n in range(4):
                if m >= 1 and n == 0:
                    t.wait_ge(am.h, t_wo_ev[m - 1])
                for K in range(MD):
                    mmi = t.matmul(ps4c[n][:, :], zT_b[:, gi % 2, K * 512 + mi * 128:K * 512 + mi * 128 + 128],
                                   wo_b[:, K * D + n * 512:K * D + (n + 1) * 512],
                                   start=(K == 0), stop=(K == MD - 1))
                t_wo_mm[(gi, mi, n)] = t_inc(mmi, tm)

        o = (m % 2) * 24
        toks_wo = dict(t_wo_mm)

        tx2p = t_x2c.get(m - 2, 0)

        def f_ev2(s, gi=gi, mi=mi, m=m, o=o, toks_wo=toks_wo, tx2p=tx2p):
            if tx2p:
                s.wait_ge(vm.h, tx2p)
            for n in range(4):
                s.wait_ge(tm.h, toks_wo[(gi, mi, n)])
                s.activation(out=at_sb[:, m % 2, n * 512:(n + 1) * 512], in_=ps4c[n][:, :], func=AF.Copy,
                             accum_out=st2[:, o + n:o + n + 1])
                mmi = s.activation(out=dumpA[:, m % 2, n * 512:(n + 1) * 512], in_=ps4c[n][:, :], func=AF.Square,
                                   accum_out=st2[:, o + 4 + n:o + 5 + n])
            mmi.then_inc(am.h, 1)
        t_wo_ev[m] = sadd(f_ev2)

    def s_stats2(m):
        o = (m % 2) * 24

        @blk.vector
        def _(v, m=m, o=o):
            v.wait_ge(am.h, t_wo_ev[m])
            v.drain()
            v.tensor_reduce(out=st2[:, o + 8:o + 9], in_=st2[:, o:o + 4], axis=AX.X, op=ALU.add)
            v.tensor_reduce(out=st2[:, o + 9:o + 10], in_=st2[:, o + 4:o + 8], axis=AX.X, op=ALU.add)
            v.drain()
            v.tensor_scalar_mul(out=st2[:, o + 8:o + 9], in0=st2[:, o + 8:o + 9], scalar1=1.0 / D)
            v.drain()
            v.tensor_tensor(out=st2[:, o + 10:o + 11], in0=st2[:, o + 8:o + 9], in1=st2[:, o + 8:o + 9], op=ALU.mult)
            v.drain()
            t_ln2[m] = t_inc(v.tensor_scalar(out=st2[:, o + 9:o + 10], in0=st2[:, o + 9:o + 10], scalar1=1.0 / D, scalar2=st2[:, o + 10:o + 11], op0=ALU.mult, op1=ALU.subtract), vm)

        def f_ln2(s, m=m, o=o, tk=t_ln2[m]):
            s.wait_ge(vm.h, tk)
            s.activation(out=st2[:, o + 11:o + 12], in_=st2[:, o + 9:o + 10], func=AF.Ln, bias=epsc[:, 0:1])
            s.drain()
            s.activation(out=st2[:, o + 12:o + 13], in_=st2[:, o + 11:o + 12], func=AF.Exp, scale=epsc[:, 2:3]).then_inc(am.h, 1)
        t_ln2b[m] = sadd(f_ln2)

    def s_apply2(m):
        o = (m % 2) * 24

        @blk.vector
        def _(v, m=m, o=o):
            v.wait_ge(am.h, t_ln2b[m])
            wt(v, t_x2ld[m])
            if m >= 2:
                wt(v, t_x2sp[m - 2])
            v.drain()
            v.tensor_scalar(out=dump3[:, m % 2, :], in0=at_sb[:, m % 2, :], scalar1=st2[:, o + 8:o + 9], scalar2=st2[:, o + 12:o + 13], op0=ALU.subtract, op1=ALU.mult)
            v.drain()
            t_x2c[m] = t_inc(v.tensor_tensor(out=x2_sb[:, m % 2, :], in0=dump3[:, m % 2, :], in1=x_b2[:, m % 2, :], op=ALU.add), vm)

        def f_x2s(s, m=m, o=o, tk=t_x2c[m]):
            s.wait_ge(vm.h, tk)
            s.activation(out=dumpA[:, m % 2, :], in_=x2_sb[:, m % 2, :], func=AF.Copy, accum_out=st2[:, o + 14:o + 15])
            s.drain()
            s.activation(out=dumpA[:, m % 2, :], in_=x2_sb[:, m % 2, :], func=AF.Square, accum_out=st2[:, o + 15:o + 16]).then_inc(am.h, 1)
        t_ln3s[m] = sadd(f_x2s)

    def s_stats3(m):
        o = (m % 2) * 24

        @blk.vector
        def _(v, m=m, o=o):
            v.wait_ge(am.h, t_ln3s[m])
            v.drain()
            v.tensor_scalar_mul(out=st2[:, o + 16:o + 17], in0=st2[:, o + 14:o + 15], scalar1=1.0 / D)
            v.drain()
            v.tensor_tensor(out=st2[:, o + 17:o + 18], in0=st2[:, o + 16:o + 17], in1=st2[:, o + 16:o + 17], op=ALU.mult)
            v.drain()
            t_ln3a[m] = t_inc(v.tensor_scalar(out=st2[:, o + 15:o + 16], in0=st2[:, o + 15:o + 16], scalar1=1.0 / D, scalar2=st2[:, o + 17:o + 18], op0=ALU.mult, op1=ALU.subtract), vm)

        def f_ln3(s, m=m, o=o, tk=t_ln3a[m]):
            s.wait_ge(vm.h, tk)
            s.activation(out=st2[:, o + 18:o + 19], in_=st2[:, o + 15:o + 16], func=AF.Ln, bias=epsc[:, 0:1])
            s.drain()
            s.activation(out=st2[:, o + 19:o + 20], in_=st2[:, o + 18:o + 19], func=AF.Exp, scale=epsc[:, 2:3]).then_inc(am.h, 1)
        t_ln3b[m] = sadd(f_ln3)

    def s_qa(m):
        o = (m % 2) * 24

        @blk.vector
        def _(v, m=m, o=o):
            v.wait_ge(am.h, t_ln3b[m])
            v.drain()
            if m >= 2:
                v.wait_ge(tm.h, t_qyt_tr.get((m - 2, 3), 0))
            b2 = m % 2
            g_ = gq2[:, b2, :]
            v.tensor_scalar(out=y_sb[:, b2, :], in0=x2_sb[:, b2, :], scalar1=st2[:, o + 16:o + 17], scalar2=st2[:, o + 19:o + 20], op0=ALU.subtract, op1=ALU.mult)
            v.drain()
            v.tensor_reduce(out=g_[:, 0:32], in_=y_sb[:, b2, :].rearrange("p (gg k) -> p gg k", k=64), axis=AX.X, op=ALU.max, apply_absolute_value=True)
            v.drain()
            v.tensor_scalar_max(out=g_[:, 0:32], in0=g_[:, 0:32], scalar1=EPS_QA)
            v.drain()
            v.reciprocal(out=g_[:, 32:64], in_=g_[:, 0:32])
            v.drain()
            v.tensor_scalar_mul(out=g_[:, 64:96], in0=g_[:, 0:32], scalar1=1.0 / 127.0)
            v.scalar_tensor_tensor(out=dump3[:, b2, :].rearrange("p (gg k) -> p gg k", k=64), in0=y_sb[:, b2, :].rearrange("p (gg k) -> p gg k", k=64),
                                   scalar=127.0, in1=g_[:, 32:64].unsqueeze(2).broadcast_to([128, 32, 64]), op0=ALU.mult, op1=ALU.mult)
            v.drain()
            v.tensor_scalar(out=dump3[:, b2, :], in0=dump3[:, b2, :], scalar1=MAGIC, scalar2=MAGIC, op0=ALU.add, op1=ALU.subtract)
            v.drain()
            t_qy[m] = t_inc(v.tensor_tensor(out=qy_bf[:, b2, :].rearrange("p (gg k) -> p gg k", k=64),
                                            in0=dump3[:, b2, :].rearrange("p (gg k) -> p gg k", k=64),
                                            in1=g_[:, 64:96].unsqueeze(2).broadcast_to([128, 32, 64]), op=ALU.mult), vm)

        @blk.sync
        def _(sy, m=m):
            sy.wait_ge(vm.h, t_qy[m])
            t_x2sp[m] = dma16(sy, SXP, out=D_x2[m * 128:(m + 1) * 128, :], in_=x2_sb[:, m % 2, :])
        if m >= 1:
            emit_qyT(m - 1)

    stages = [s_qa, s_stats3, s_apply2, s_stats2, s_mm]
    for k in range(MT + len(stages) - 1):
        for lag, st_fn in enumerate(stages):
            mm_ = k - (len(stages) - 1 - lag)
            if 0 <= mm_ < MT:
                st_fn(mm_)
    emit_qyT(MT - 1)


    barrier()
    if upto <= 3:
        e2c.close(); ctx.close()
        return nc
    e2c.close()

    e3 = contextlib.ExitStack()
    qyT = e3.enter_context(sb("qyT", [128, MD * T], BF16))
    w12_b = e3.enter_context(sb("w12_b", [128, 2, 2, MD * 512], BF16))
    silu_sb = e3.enter_context(sb("silu_sb", [128, 2, 512], F32))
    h_sb = e3.enter_context(sb("h_sb", [128, 2, 512], F32))
    qh_bf = e3.enter_context(sb("qh_bf", [128, 3, 512], BF16))
    gq3 = e3.enter_context(sb("gq3", [128, 2, 24], F32))
    tq3 = e3.enter_context(sb("tq3", [128, 2, 512], F32))

    ps3 = [e3.enter_context(nc.psum_tensor(f"ps_e3_{i}", [128, 512], F32)) for i in range(4)]
    ps_t3 = [e3.enter_context(nc.psum_tensor(f"ps_t3_{i}", [128, 512], BF16)) for i in range(2)]
    t3_st = e3.enter_context(sb("t3_st", [128, 2, 512], BF16))
    w3tmp = e3.enter_context(sb("w3tmp", [128, 1], F32))

    t_qyT_d = {}

    @blk.gpsimd
    def _(g_):
        t_qyT_d["tok"] = dma16(g_, GXP, out=qyT[:, :].rearrange("p (kc t2) -> p kc t2", t2=T),
                               in_=D_qayT[:, :].rearrange("(kc p) t2 -> p kc t2", p=128))
    t_qyT = t_qyT_d["tok"]

    NSLAB = 16
    w3p = weight_prep_make(e3, "w3", w3T, D_w3s, DFF, D, 2048, 0)
    for _pf in range(w3p["nslot"]):
        w3p["emit_dma"](_pf)
    t_w12, t_3mm, t_3sl, t_3h, t_3qh, t_3sp, t_3tr, t_3te = {}, {}, {}, {}, {}, {}, {}, {}

    @blk.sync
    def _(sy):
        for sl in range(2):
            for w in range(2):
                src = [D_w1s, D_w2s][w]
                dma16(sy, SXP, out=w12_b[:, sl % 2, w, :].rearrange("p (kc n2) -> p kc n2", n2=512),
                                   in_=src[:, sl * 512:(sl + 1) * 512].rearrange("(kc p) n2 -> p kc n2", p=128))
            t_w12[sl] = snap(SXP)

    for sl in range(NSLAB):
        for m in range(MT):
            idx = sl * MT + m
            bs = (idx % 2) * 2
            b = idx % 2

            @blk.tensor
            def _(t, sl=sl, m=m, idx=idx, bs=bs):
                if m == 0:
                    wt(t, t_w12[sl])
                    if sl == 0:
                        wt(t, t_qyT)
                if idx >= 2:
                    pidx = idx - 2
                    t.wait_ge(vm.h, t_3h[pidx])
                for w in range(2):
                    for K in range(MD):
                        mmi = t.matmul(ps3[bs + w][:, :], qyT[:, K * T + m * 128:K * T + m * 128 + 128],
                                       w12_b[:, sl % 2, w, K * 512:(K + 1) * 512],
                                       start=(K == 0), stop=(K == MD - 1))
                t_3mm[idx] = t_inc(mmi, tm)

            def f_sl(s, idx=idx, bs=bs, b=b, tmm=t_3mm[idx], th=t_3h.get(idx - 2)):
                s.wait_ge(tm.h, tmm)
                if th:
                    s.wait_ge(vm.h, th)
                s.activation(out=silu_sb[:, b, :], in_=ps3[bs][:, :], func=AF.Silu, scale=wscale[:, 0:1]).then_inc(am.h, 1)
            t_3sl[idx] = sadd(f_sl)

            @blk.vector
            def _(v, idx=idx, bs=bs, b=b, m=m, sl=sl):
                v.wait_ge(am.h, t_3sl[idx])
                v.drain()
                if idx >= 3:
                    v.wait_ge(tm.h, t_3tr[idx - 3])
                v.scalar_tensor_tensor(out=h_sb[:, b, :], in0=ps3[bs + 1][:, :], scalar=wscale[:, 1:2], in1=silu_sb[:, b, :], op0=ALU.mult, op1=ALU.mult)
                v.drain()
                t_3h[idx] = t_inc(v.engine_nop(), vm)
                v.tensor_reduce(out=gq3[:, b, 0:8], in_=h_sb[:, b, :].rearrange("p (gg k) -> p gg k", k=64), axis=AX.X, op=ALU.max, apply_absolute_value=True)
                v.drain()
                v.tensor_scalar_max(out=gq3[:, b, 0:8], in0=gq3[:, b, 0:8], scalar1=EPS_QA)
                v.drain()
                v.reciprocal(out=gq3[:, b, 8:16], in_=gq3[:, b, 0:8])
                v.drain()
                v.tensor_scalar_mul(out=gq3[:, b, 16:24], in0=gq3[:, b, 0:8], scalar1=1.0 / 127.0)
                v.scalar_tensor_tensor(out=tq3[:, b, :].rearrange("p (gg k) -> p gg k", k=64), in0=h_sb[:, b, :].rearrange("p (gg k) -> p gg k", k=64),
                                       scalar=127.0, in1=gq3[:, b, 8:16].unsqueeze(2).broadcast_to([128, 8, 64]), op0=ALU.mult, op1=ALU.mult)
                v.drain()
                v.tensor_scalar(out=tq3[:, b, :], in0=tq3[:, b, :], scalar1=MAGIC, scalar2=MAGIC, op0=ALU.add, op1=ALU.subtract)
                v.drain()
                t_3qh[idx] = t_inc(v.tensor_tensor(out=qh_bf[:, idx % 3, :].rearrange("p (gg k) -> p gg k", k=64),
                                                   in0=tq3[:, b, :].rearrange("p (gg k) -> p gg k", k=64),
                                                   in1=gq3[:, b, 16:24].unsqueeze(2).broadcast_to([128, 8, 64]), op=ALU.mult), vm)

            def emit_qh_tr(idx2):
                if idx2 < 0 or idx2 >= NSLAB * MT:
                    return
                sl2, m2 = idx2 // MT, idx2 % MT
                s3_ = idx2 % 3
                b2 = idx2 % 2

                @blk.tensor
                def _(t, idx2=idx2, b2=b2, s3_=s3_):
                    t.wait_ge(vm.h, t_3qh[idx2])
                    prev = t_3te.get(idx2 - 2, 0)
                    if prev:
                        t.wait_ge(vm.h, prev)
                    for i in range(4):
                        mmi = t.transpose(ps_t3[b2][:, i * 128:(i + 1) * 128], qh_bf[:, s3_, i * 128:(i + 1) * 128], ident[:, :])
                    t_3tr[idx2] = t_inc(mmi, tm)

                @blk.vector
                def _(v, idx2=idx2, b2=b2):
                    v.wait_ge(tm.h, t_3tr[idx2])
                    if idx2 >= 2:
                        wt(v, t_3sp[idx2 - 2])
                    t_3te[idx2] = t_inc(v.tensor_copy(out=t3_st[:, b2, :], in_=ps_t3[b2][:, :]), vm)

                @blk.sync
                def _(sy, idx2=idx2, sl2=sl2, m2=m2, b2=b2):
                    sy.wait_ge(vm.h, t_3te[idx2])
                    t_3sp[idx2] = dma16(sy, SXP,
                        out=D_qahT[sl2 * 512:(sl2 + 1) * 512, m2 * 128:(m2 + 1) * 128].rearrange("(i p) t2 -> p i t2", p=128),
                        in_=t3_st[:, b2, :].rearrange("p (i t2) -> p i t2", t2=128))
            emit_qh_tr(idx - 2)
            if idx == NSLAB * MT - 1:
                emit_qh_tr(idx - 1)
                emit_qh_tr(idx)

            if idx % 4 == 0:
                w3p["emit_chunk"](idx // 4)

        if sl + 2 < NSLAB:
            @blk.sync
            def _(sy, sl=sl):
                sy.wait_ge(tm.h, t_3mm[sl * MT + MT - 1])
                for w in range(2):
                    src = [D_w1s, D_w2s][w]
                    dma16(sy, SXP, out=w12_b[:, sl % 2, w, :].rearrange("p (kc n2) -> p kc n2", n2=512),
                                       in_=src[:, (sl + 2) * 512:(sl + 3) * 512].rearrange("(kc p) n2 -> p kc n2", p=128))
                t_w12[sl + 2] = snap(SXP)

    finish_scale(DFF * D, 2, w3p["abs_last"](), w3p["nchunk"], 0, ps3[0][0:1, 0:1], ps3[1][:, 0:1], w3tmp[:, :],
                 pe_wait_vm=t_3h[NSLAB * MT - 1], pe_wait_am=t_3sl[NSLAB * MT - 1])

    barrier()
    if upto <= 4:
        e3.close(); ctx.close()
        return nc
    e3.close()

    e4 = contextlib.ExitStack()
    w3q = e4.enter_context(sb("w3q", [128, 2, MF * 512], BF16))
    qhT = e4.enter_context(sb("qhT", [128, 2, MF * 256], BF16))
    pre4 = e4.enter_context(sb("pre4", [128, 2, 512], BF16))
    ps_a = [e4.enter_context(nc.psum_tensor(f"ps_e4_{i}", [128, 512], F32)) for i in range(4)]
    ps_dmp = e4.enter_context(nc.psum_tensor("ps_dmp", [128, 512], F32))

    t_w3q, t_qhT, t_4mm, t_4ev, t_4sp = {}, {}, {}, {}, {}

    for q in range(4):
        @blk.gpsimd
        def _(g_, q=q):
            if q >= 2:
                g_.wait_ge(tm.h, t_4mm[(q - 2, 15)])
            t_w3q[q] = dma16(g_, GXP, out=w3q[:, q % 2, :].rearrange("p (kc n2) -> p kc n2", n2=512),
                             in_=D_w3s[:, q * 512:(q + 1) * 512].rearrange("(kc p) n2 -> p kc n2", p=128))

        def emit_qhT_load(g2, q=q):
            @blk.gpsimd
            def _(g_, q=q, g2=g2):
                pg = t_4mm.get((q, (g2 - 2) * 2 + 1), t_4mm.get((q - 1, (6 + g2 % 2) * 2 + 1), None))
                if pg is not None:
                    g_.wait_ge(tm.h, pg)
                t_qhT[(q, g2)] = dma16(g_, GXP,
                    out=qhT[:, g2 % 2, :].rearrange("p (kc t2) -> p kc t2", t2=256),
                    in_=D_qahT[:, g2 * 256:(g2 + 1) * 256].rearrange("(kc p) t2 -> p kc t2", p=128))
        emit_qhT_load(0)
        emit_qhT_load(1)

        for g in range(8):
            for mi in range(2):
                idx = g * 2 + mi
                m = idx
                bank = idx % 4
                b = idx % 2
                csum = 8 * m + q
                csq = 8 * m + 4 + q

                @blk.tensor
                def _(t, q=q, g=g, mi=mi, idx=idx, bank=bank):
                    if mi == 0:
                        wt(t, t_qhT[(q, g)])
                        if g == 0:
                            wt(t, t_w3q[q])
                    pidx = idx - 4
                    if pidx >= 0:
                        t.wait_ge(vm.h, t_4ev[(q, pidx)])
                    elif q >= 1:
                        t.wait_ge(vm.h, t_4ev[(q - 1, 12 + idx)])
                    for K in range(MF):
                        mmi = t.matmul(ps_a[bank][:, :], qhT[:, g % 2, K * 256 + mi * 128:K * 256 + mi * 128 + 128],
                                       w3q[:, q % 2, K * 512:(K + 1) * 512],
                                       start=(K == 0), stop=(K == MF - 1))
                    t_4mm[(q, idx)] = t_inc(mmi, tm)

                @blk.vector
                def _(v, q=q, idx=idx, bank=bank, b=b, csum=csum, csq=csq):
                    v.wait_ge(tm.h, t_4mm[(q, idx)])
                    v.drain()
                    if idx >= 2:
                        wt(v, t_4sp[(q, idx - 2)])
                    elif q >= 1:
                        wt(v, t_4sp[(q - 1, 14 + idx)])
                    v.tensor_scalar(out=pre4[:, b, :], in0=ps_a[bank][:, :], scalar1=wscale[:, 2:3], scalar2=0.0,
                                    op0=ALU.mult, op1=ALU.add, accum_out=ln4st[:, csum:csum + 1])
                    v.drain()
                    v.scalar_tensor_tensor(out=ps_dmp[:, :], in0=pre4[:, b, :], scalar=0.0, in1=pre4[:, b, :],
                                           op0=ALU.subtract, op1=ALU.mult, accum_out=ln4st[:, csq:csq + 1])
                    t_4ev[(q, idx)] = t_inc(v.engine_nop(), vm)

                @blk.sync
                def _(sy, q=q, m=m, idx=idx, b=b):
                    sy.wait_ge(vm.h, t_4ev[(q, idx)])
                    t_4sp[(q, idx)] = dma16(sy, SXP, out=D_pre4[m * 128:(m + 1) * 128, q * 512:(q + 1) * 512],
                                            in_=pre4[:, b, :])

            if mi == 1 and g + 2 < 8:
                emit_qhT_load(g + 2)

    barrier()
    if upto <= 5:
        e4.close(); ctx.close()
        return nc
    e4.close()

    e5 = contextlib.ExitStack()
    p4_b = e5.enter_context(sb("p4_b", [128, 2, D], BF16))
    x2_b = e5.enter_context(sb("x2_b", [128, 2, D], F32))
    o_f = e5.enter_context(sb("o_f", [128, 2, D], F32))
    st5 = e5.enter_context(sb("st5", [128, 16], F32))

    t_5ld, t_5c, t_5st = {}, {}, {}

    for m in range(MT):
        o = (m % 2) * 8

        @blk.sync
        def _(sy, m=m):
            if m >= 2:
                sy.wait_ge(vm.h, t_5c[m - 2])
            tok_a = dma16(sy, SXP, out=p4_b[:, m % 2, :], in_=D_pre4[m * 128:(m + 1) * 128, :])
            tok_b = dma16(sy, SXP, out=x2_b[:, m % 2, :], in_=D_x2[m * 128:(m + 1) * 128, :])
            t_5ld[m] = [tok_a, tok_b]

        @blk.vector
        def _(v, m=m, o=o):
            wt(v, t_5ld[m])
            v.drain()
            if m >= 2:
                wt(v, t_5st[m - 2])
            v.tensor_reduce(out=st5[:, o:o + 1], in_=ln4st[:, 8 * m:8 * m + 4], axis=AX.X, op=ALU.add)
            v.tensor_reduce(out=st5[:, o + 1:o + 2], in_=ln4st[:, 8 * m + 4:8 * m + 8], axis=AX.X, op=ALU.add)
            v.drain()
            v.tensor_scalar_mul(out=st5[:, o:o + 1], in0=st5[:, o:o + 1], scalar1=1.0 / D)
            v.tensor_scalar_mul(out=st5[:, o + 1:o + 2], in0=st5[:, o + 1:o + 2], scalar1=1.0 / D)
            v.drain()
            v.tensor_tensor(out=st5[:, o + 2:o + 3], in0=st5[:, o:o + 1], in1=st5[:, o:o + 1], op=ALU.mult)
            v.drain()
            t_inc(v.tensor_scalar(out=st5[:, o + 3:o + 4], in0=st5[:, o + 1:o + 2], scalar1=st5[:, o + 2:o + 3], scalar2=EPS_LN, op0=ALU.subtract, op1=ALU.add), vm)
        t_5a = vm.n

        def f_5(s, o=o, t_5a=t_5a):
            s.wait_ge(vm.h, t_5a)
            s.activation(out=st5[:, o + 4:o + 5], in_=st5[:, o + 3:o + 4], func=AF.Ln)
            s.drain()
            s.activation(out=st5[:, o + 5:o + 6], in_=st5[:, o + 4:o + 5], func=AF.Exp, scale=epsc[:, 2:3]).then_inc(am.h, 1)
        t_5b = sadd(f_5)

        @blk.vector
        def _(v, m=m, o=o, t_5b=t_5b):
            v.wait_ge(am.h, t_5b)
            v.drain()
            v.tensor_scalar(out=o_f[:, m % 2, :], in0=p4_b[:, m % 2, :], scalar1=st5[:, o:o + 1], scalar2=st5[:, o + 5:o + 6], op0=ALU.subtract, op1=ALU.mult)
            v.drain()
            t_5c[m] = t_inc(v.tensor_tensor(out=o_f[:, m % 2, :], in0=o_f[:, m % 2, :], in1=x2_b[:, m % 2, :], op=ALU.add), vm)

        @blk.sync
        def _(sy, m=m):
            sy.wait_ge(vm.h, t_5c[m])
            t_5st[m] = dma16(sy, SXP, out=out[m * 128:(m + 1) * 128, :], in_=o_f[:, m % 2, :])

    barrier()
    e5.close()
    ctx.close()
    return nc


_NC_CACHE = {}


def _get_nc(debug=False):
    key = bool(debug)
    if key not in _NC_CACHE:
        _NC_CACHE[key] = build_nc(debug=key)
    return _NC_CACHE[key]


def make_in_maps(L0_attn_x, L0_attn_slope_rate, L0_attn_W_qkvu, L0_attn_W_out,
                 L0_ffn_w1, L0_ffn_w2, L0_ffn_w3):
    x = np.asarray(L0_attn_x, dtype=np.float32)
    Wqkvu = np.asarray(L0_attn_W_qkvu, dtype=np.float32)
    wqkT_a = np.ascontiguousarray(Wqkvu[0:2 * D, :].T)
    wvuT_a = np.ascontiguousarray(Wqkvu[2 * D:4 * D, :].T)
    woT_a = np.ascontiguousarray(np.asarray(L0_attn_W_out, dtype=np.float32).T)
    w1T_a = np.ascontiguousarray(np.asarray(L0_ffn_w1, dtype=np.float32).T)
    w2T_a = np.ascontiguousarray(np.asarray(L0_ffn_w2, dtype=np.float32).T)
    w3T_a = np.ascontiguousarray(np.asarray(L0_ffn_w3, dtype=np.float32).T)
    slope = np.asarray(L0_attn_slope_rate, dtype=np.float32).reshape(H)
    slope_ba = np.ascontiguousarray(np.broadcast_to(slope[None, :], (128, H)))

    in_maps = []
    for c in range(NC):
        b, half = c // 2, c % 2
        xs = np.ascontiguousarray(x[b, half * T:(half + 1) * T, :])
        mask = np.zeros((128, NC), np.float32)
        if half == 1:
            mask[:, c - 1] = 1.0
        in_maps.append({
            "x_in": xs, "wqkT": wqkT_a, "wvuT": wvuT_a, "woT": woT_a,
            "w1T": w1T_a, "w2T": w2T_a, "w3T": w3T_a,
            "slope_b": slope_ba, "kvmask": np.ascontiguousarray(mask),
        })
    return in_maps


def kernel(L0_attn_x, L0_attn_slope_rate, L0_attn_W_qkvu, L0_attn_W_out,
           L0_ffn_w1, L0_ffn_w2, L0_ffn_w3):
    in_maps = make_in_maps(L0_attn_x, L0_attn_slope_rate, L0_attn_W_qkvu, L0_attn_W_out,
                           L0_ffn_w1, L0_ffn_w2, L0_ffn_w3)
    nc = _get_nc(debug=False)
    res = run_bass_kernel_spmd(nc, in_maps, core_ids=list(range(NC)))
    outp = np.empty((4, 4096, D), dtype=np.float32)
    for c in range(NC):
        b, half = c // 2, c % 2
        outp[b, half * T:(half + 1) * T, :] = res.results[c]["out"]
    return outp


if __name__ == "__main__":
    nco = build_nc()
    print("built ok; instructions:", nco.n_instructions())
    print("sbuf bytes remaining/partition:", nco.sbuf_bytes_remaining)


# version 14
# speedup vs baseline: 1.0877x; 1.0877x over previous
import contextlib
import os
import numpy as np
import ml_dtypes
import concourse.bass as bass
import concourse.mybir as mybir
from concourse.bass_utils import run_bass_kernel_spmd

F32 = mybir.dt.float32
BF16 = mybir.dt.bfloat16
AF = mybir.ActivationFunctionType
ALU = mybir.AluOpType
AX = mybir.AxisListType

T = 2048
D = 2048
H = 16
HD = 128
BLK = 256
NBLK = T // BLK
DFF = 8192
NC = 8
MAGIC = 12582912.0
EPS_LN = 1e-5
EPS_RMS = 1e-6
EPS_QA = 1e-5

MT = T // 128
MD = D // 128
MF = DFF // 128


def build_nc(debug=False, upto=99):
    nc = bass.Bass("TRN2", target_bir_lowering=False)
    ctx = contextlib.ExitStack()

    x_in = nc.dram_tensor("x_in", [T, D], F32, kind="ExternalInput").ap()
    wqkT = nc.dram_tensor("wqkT", [D, 2 * D], F32, kind="ExternalInput").ap()
    wvuT = nc.dram_tensor("wvuT", [D, 2 * D], F32, kind="ExternalInput").ap()
    woT = nc.dram_tensor("woT", [D, D], F32, kind="ExternalInput").ap()
    w1T = nc.dram_tensor("w1T", [D, DFF], F32, kind="ExternalInput").ap()
    w2T = nc.dram_tensor("w2T", [D, DFF], F32, kind="ExternalInput").ap()
    w3T = nc.dram_tensor("w3T", [DFF, D], F32, kind="ExternalInput").ap()
    slope_b = nc.dram_tensor("slope_b", [128, H], F32, kind="ExternalInput").ap()
    kvmask = nc.dram_tensor("kvmask", [128, NC], F32, kind="ExternalInput").ap()
    out = nc.dram_tensor("out", [T, D], F32, kind="ExternalOutput").ap()

    dkind = "ExternalOutput" if debug else "Internal"
    D_xn1 = nc.dram_tensor("D_xn1", [T, D], BF16, kind=dkind).ap()
    D_qfm = nc.dram_tensor("D_qfm", [D, T], BF16, kind=dkind).ap()
    D_kfm = nc.dram_tensor("D_kfm", [D, T], BF16, kind=dkind).ap()
    D_v = nc.dram_tensor("D_v", [T, D], BF16, kind=dkind).ap()
    D_u = nc.dram_tensor("D_u", [T, D], BF16, kind=dkind).ap()
    D_z = nc.dram_tensor("D_z", [T, D], BF16, kind=dkind).ap()
    D_zT = nc.dram_tensor("D_zT", [D, T], BF16, kind=dkind).ap()
    D_x2 = nc.dram_tensor("D_x2", [T, D], F32, kind=dkind).ap()
    D_qayT = nc.dram_tensor("D_qayT", [D, T], BF16, kind=dkind).ap()
    D_qah = nc.dram_tensor("D_qah", [T, DFF], BF16, kind=dkind).ap()
    D_qahT = nc.dram_tensor("D_qahT", [DFF, T], BF16, kind=dkind).ap()
    D_pre4 = nc.dram_tensor("D_pre4", [T, D], BF16, kind=dkind).ap()
    D_w1s = nc.dram_tensor("D_w1s", [D, DFF], BF16).ap()
    D_w2s = nc.dram_tensor("D_w2s", [D, DFF], BF16).ap()
    D_w3s = nc.dram_tensor("D_w3s", [DFF, D], BF16).ap()
    D_kv0 = nc.dram_tensor("D_kv0", [128, H * HD], F32, kind=dkind).ap()
    D_cci = nc.dram_tensor("D_cci", [H * HD, HD], F32)
    D_cco = nc.dram_tensor("D_cco", [NC * H * HD, HD], F32)

    ident_t = nc.inline_tensor(np.eye(128, dtype=ml_dtypes.bfloat16), "ident")
    ones_col_t = nc.inline_tensor(np.ones((128, 1), np.float32), "ones_col")
    ones_row_t = nc.inline_tensor(np.ones((1, 128), np.float32), "ones_row")
    m_ar = np.arange(BLK, dtype=np.float32)
    dargs = np.stack([m_ar + 1.0, 255.0 - m_ar, np.full(BLK, 256.0, np.float32)], axis=1)
    dargs3_t = nc.inline_tensor(np.ascontiguousarray(dargs), "dargs3")
    idx = m_ar[None, :] - m_ar[:, None]
    idxT = np.where(idx >= 0.0, idx, 1e9).astype(np.float32)
    idxT_t = nc.inline_tensor(np.ascontiguousarray(idxT), "idxT")
    posr_t = nc.inline_tensor(np.ascontiguousarray(np.broadcast_to(np.arange(1, 257, dtype=np.float32), (128, 256))), "posr")

    class S:
        def __init__(self, name):
            self.h = ctx.enter_context(nc.semaphore(name))
            self.n = 0

    class Pool:
        def __init__(self, name, n=8):
            self.sems = [S(f"{name}{i}") for i in range(n)]
            self.i = 0

    SXP = Pool("sx")
    GXP = Pool("gx")
    cc = S("cc")
    tm = S("tm")
    vm = S("vm")
    am = S("am")
    pm = S("pm")
    bar = S("bar")

    blk = ctx.enter_context(nc.Block())
    sb = nc.sbuf_tensor

    def t_inc(instr, s, k=1):
        instr.then_inc(s.h, k)
        s.n += k
        return s.n

    def dma16(eng, pool, **kw):
        s_ = pool.sems[pool.i % len(pool.sems)]
        pool.i += 1
        if s_.n >= 16:
            eng.wait_ge(s_.h, s_.n)
        ins = eng.dma_start(**kw)
        ins.then_inc(s_.h, 16)
        s_.n += 16
        return (s_.h, s_.n)

    def snap(pool):
        return [(s_.h, s_.n) for s_ in pool.sems]

    def wt(eng, tok):
        if isinstance(tok, list):
            for hh, vv in tok:
                eng.wait_ge(hh, vv)
        else:
            eng.wait_ge(tok[0], tok[1])

    sd_ops = []

    def sadd(fn, ninc=1):
        am.n += ninc
        sd_ops.append(fn)
        return am.n

    def sflush():
        if not sd_ops:
            return
        ops = list(sd_ops)
        sd_ops.clear()

        @blk.scalar
        def _(s):
            for f in ops:
                f(s)

    def barrier():
        sflush()
        snaps = snap(SXP) + snap(GXP) + [(tm.h, tm.n), (vm.h, vm.n), (am.h, am.n)]

        def wait_all(e):
            for hh, vv in snaps:
                e.wait_ge(hh, vv)

        @blk.sync
        def _(sy):
            wait_all(sy)

        @blk.gpsimd
        def _(g):
            wait_all(g)

        @blk.tensor
        def _(t):
            wait_all(t)

        @blk.vector
        def _(v):
            wait_all(v)

        @blk.scalar
        def _(s):
            wait_all(s)

    ident = ctx.enter_context(sb("ident_s", [128, 128], BF16))
    ones_col = ctx.enter_context(sb("ones_col_s", [128, 1], F32))
    ones_row = ctx.enter_context(sb("ones_row_s", [1, 128], F32))
    slope_s = ctx.enter_context(sb("slope_s", [128, H], F32))
    nslope = ctx.enter_context(sb("nslope", [128, H], F32))
    kvmask_s = ctx.enter_context(sb("kvmask_s", [128, NC], F32))
    eA = contextlib.ExitStack()
    dargs3 = eA.enter_context(sb("dargs3_s", [128, 2, 3], F32, side="right"))
    idxT_s = eA.enter_context(sb("idxT_s", [128, 2, 256], F32, side="right"))
    qdec = eA.enter_context(sb("qdec", [128, 2, H], F32, side="right"))
    kdec = eA.enter_context(sb("kdec", [128, 2, H], F32, side="right"))
    bdec = eA.enter_context(sb("bdec", [128, H], F32, side="right"))
    decayT = eA.enter_context(sb("decayT", [128, H, 2, 256], BF16, side="right"))
    posr_s = eA.enter_context(sb("posr_s", [128, 256], F32, side="right"))
    qdecT = eA.enter_context(sb("qdecT", [128, H, 256], BF16, side="right"))
    wscale = ctx.enter_context(sb("wscale", [128, 4], F32))
    wabs = ctx.enter_context(sb("wabs", [128, 64], F32))
    scal_1 = ctx.enter_context(sb("scal_1", [1, 1], F32))
    ln4st = ctx.enter_context(sb("ln4st", [128, 128], F32))
    epsc = ctx.enter_context(sb("epsc", [128, 4], F32))

    @blk.sync
    def _(sy):
        for dst, src in [(ident[:, :], ident_t.ap()[:, :]),
                         (ones_col[:, :], ones_col_t.ap()[:, :]),
                         (ones_row[:, :], ones_row_t.ap()[:, :]),
                         (slope_s[:, :], slope_b[:, :]),
                         (kvmask_s[:, :], kvmask[:, :])]:
            dma16(sy, SXP, out=dst, in_=src)
        dma16(sy, SXP, out=dargs3[:, :, :], in_=dargs3_t.ap().rearrange("(c p) j -> p c j", p=128))
        dma16(sy, SXP, out=idxT_s[:, :, :], in_=idxT_t.ap().rearrange("(c p) m -> p c m", p=128))
        dma16(sy, SXP, out=posr_s[:, :], in_=posr_t.ap()[:, :])
    prep_loads = snap(SXP)

    @blk.vector
    def _(v):
        wt(v, prep_loads)
        v.tensor_scalar_mul(out=nslope[:, :], in0=slope_s[:, :], scalar1=-1.0)
        v.memset(epsc[:, 0:1], EPS_LN)
        v.memset(epsc[:, 1:2], EPS_RMS)
        v.memset(epsc[:, 2:3], -0.5)
        v.drain()
        for c in range(2):
            v.tensor_tensor(out=qdec[:, c, :], in0=nslope[:, :], in1=dargs3[:, c, 0:1].to_broadcast([128, H]), op=ALU.mult)
            v.tensor_tensor(out=kdec[:, c, :], in0=nslope[:, :], in1=dargs3[:, c, 1:2].to_broadcast([128, H]), op=ALU.mult)
        v.tensor_tensor(out=bdec[:, :], in0=nslope[:, :], in1=dargs3[:, 0, 2:3].to_broadcast([128, H]), op=ALU.mult)
        for h in range(H):
            for c in range(2):
                v.tensor_scalar(out=decayT[:, h, c, :], in0=idxT_s[:, c, :], scalar1=nslope[:, h:h + 1], scalar2=None, op0=ALU.mult)
            v.tensor_scalar(out=qdecT[:, h, :], in0=posr_s[:, :], scalar1=nslope[:, h:h + 1], scalar2=None, op0=ALU.mult)
        t_inc(v.engine_nop(), vm)
    prep_v = vm.n

    @blk.scalar
    def _(s):
        s.wait_ge(vm.h, prep_v)
        for c in range(2):
            s.activation(out=qdec[:, c, :], in_=qdec[:, c, :], func=AF.Exp)
            s.activation(out=kdec[:, c, :], in_=kdec[:, c, :], func=AF.Exp)
        s.activation(out=bdec[:, :], in_=bdec[:, :], func=AF.Exp)
        last = None
        for h in range(H):
            for c in range(2):
                last = s.activation(out=decayT[:, h, c, :], in_=decayT[:, h, c, :], func=AF.Exp)
            last = s.activation(out=qdecT[:, h, :], in_=qdecT[:, h, :], func=AF.Exp)
        t_inc(last, am)
    prep_done = am.n

    def weight_prep_make(estack, name, src_ap, dst_ap, rows, cols, colw, acc_off, nslot=3):
        nrow = rows // 128
        ncsp = cols // colw
        nchunk = nrow * ncsp
        b_in = estack.enter_context(sb(f"wp_in_{name}", [128, nslot, colw], BF16))
        b_sg = estack.enter_context(sb(f"wp_sg_{name}", [128, 2, colw], BF16))
        t_in, t_sg, t_abs, t_spill = {}, {}, {}, {}

        def rng(kc):
            return (kc // ncsp) * 128, (kc % ncsp) * colw

        def emit_dma(kc):
            if kc >= nchunk:
                return

            @blk.gpsimd
            def _(g):
                if kc >= nslot:
                    g.wait_ge(am.h, t_sg[kc - nslot])
                    g.wait_ge(vm.h, t_abs[kc - nslot])
                r, c0 = rng(kc)
                t_in[kc] = dma16(g, GXP, out=b_in[:, kc % nslot, :], in_=src_ap[r:r + 128, c0:c0 + colw])

        def emit_sign(kc):
            tin = t_in[kc]
            tsp = t_spill.get(kc - 2)

            def f(s, kc=kc, tin=tin, tsp=tsp):
                wt(s, tin)
                if tsp is not None:
                    wt(s, tsp)
                s.activation(out=b_sg[:, kc % 2, :], in_=b_in[:, kc % nslot, :], func=AF.Sign).then_inc(am.h, 1)
            t_sg[kc] = sadd(f)

        def emit_abs(kc):
            @blk.vector
            def _(v):
                wt(v, t_in[kc])
                t_abs[kc] = t_inc(v.tensor_reduce(out=wabs[:, acc_off + kc:acc_off + kc + 1], in_=b_in[:, kc % nslot, :],
                                                  axis=AX.X, op=ALU.add, apply_absolute_value=True), vm)

        def emit_spill(kc):
            @blk.sync
            def _(sy):
                sy.wait_ge(am.h, t_sg[kc])
                r, c0 = rng(kc)
                t_spill[kc] = dma16(sy, SXP, out=dst_ap[r:r + 128, c0:c0 + colw], in_=b_sg[:, kc % 2, :])

        def emit_chunk(kc):
            if kc >= nchunk:
                return
            emit_sign(kc)
            emit_abs(kc)
            emit_spill(kc)
            emit_dma(kc + nslot)

        return {"emit_dma": emit_dma, "emit_chunk": emit_chunk, "nchunk": nchunk, "nslot": nslot,
                "abs_last": lambda: t_abs[nchunk - 1]}

    def finish_scale(n_elems, out_col, abs_tok, nchunk, acc_off, psA, psB, tmp_col, pe_wait_vm=0, pe_wait_am=0):
        @blk.vector
        def _(v):
            v.wait_ge(vm.h, abs_tok)
            v.tensor_reduce(out=tmp_col, in_=wabs[:, acc_off:acc_off + nchunk], axis=AX.X, op=ALU.add)
            t_inc(v.engine_nop(), vm)
        t0 = vm.n

        @blk.tensor
        def _(t):
            t.wait_ge(vm.h, t0)
            if pe_wait_vm:
                t.wait_ge(vm.h, pe_wait_vm)
            if pe_wait_am:
                t.wait_ge(am.h, pe_wait_am)
            t_inc(t.matmul(psA, ones_col[:, :], tmp_col, start=True, stop=True), tm)
        t1 = tm.n

        @blk.vector
        def _(v):
            v.wait_ge(tm.h, t1)
            t_inc(v.tensor_copy(out=scal_1[:, :], in_=psA), vm)
        t2 = vm.n

        @blk.tensor
        def _(t):
            t.wait_ge(vm.h, t2)
            t_inc(t.matmul(psB, ones_row[:, :], scal_1[:, :], start=True, stop=True), tm)
        t3 = tm.n

        @blk.vector
        def _(v):
            v.wait_ge(tm.h, t3)
            t_inc(v.tensor_scalar_mul(out=wscale[:, out_col:out_col + 1], in0=psB, scalar1=1.0 / float(n_elems)), vm)
        return vm.n

    e1 = contextlib.ExitStack()
    x_buf = e1.enter_context(sb("x_buf", [128, 2, D], F32))
    xn1_bf = e1.enter_context(sb("xn1_bf", [128, 2, D], BF16))
    stat = e1.enter_context(sb("stat", [128, 16], F32))
    dump = e1.enter_context(sb("dump", [128, D], F32))
    xn1T = e1.enter_context(sb("xn1T", [128, MD * T], BF16))
    wqk_b = e1.enter_context(sb("wqk_b", [128, 2, MD * 128], BF16))
    wvu_b = e1.enter_context(sb("wvu_b", [128, 2, MD * 256], BF16))
    qk_st = e1.enter_context(sb("qk_st", [128, 2, T], BF16))
    vu_st = e1.enter_context(sb("vu_st", [128, 2, 256], BF16))

    ps4 = [e1.enter_context(nc.psum_tensor(f"ps_e1_{i}", [128, 512], F32)) for i in range(4)]
    ps_s1 = e1.enter_context(nc.psum_tensor("ps_s1", [1, 1], F32))
    ps_s128 = e1.enter_context(nc.psum_tensor("ps_s128", [128, 1], F32))
    ps_xt = [e1.enter_context(nc.psum_tensor(f"ps_xt{i}", [128, 512], BF16)) for i in range(2)]

    t_xload, t_ln, t_sq, t_xn, t_xn_spill = {}, {}, {}, {}, {}
    t_xnT_tr, t_xnT_ev = {}, {}

    def emit_ln1_apply(m):
        o = (m % 2) * 8

        @blk.vector
        def _(v, m=m, o=o):
            v.wait_ge(am.h, t_sq[m])
            if m >= 2:
                v.wait_ge(tm.h, t_xnT_tr[(m - 2, 3)])
            t_xn[m] = t_inc(v.tensor_scalar(out=xn1_bf[:, m % 2, :], in0=x_buf[:, m % 2, :], scalar1=stat[:, o:o + 1], scalar2=stat[:, o + 4:o + 5], op0=ALU.subtract, op1=ALU.mult), vm)

        for g in range(4):
            gi = m * 4 + g
            stg = gi % 2

            @blk.tensor
            def _(t, m=m, g=g, gi=gi, stg=stg):
                if g == 0:
                    t.wait_ge(vm.h, t_xn[m])
                prev = t_xnT_ev.get(gi - 2, 0)
                if prev:
                    t.wait_ge(vm.h, prev)
                for i in range(4):
                    K = g * 4 + i
                    mmi = t.transpose(ps_xt[stg][:, i * 128:(i + 1) * 128], xn1_bf[:, m % 2, K * 128:(K + 1) * 128], ident[:, :])
                t_xnT_tr[(m, g)] = t_inc(mmi, tm)

            @blk.vector
            def _(v, m=m, g=g, gi=gi, stg=stg):
                v.wait_ge(tm.h, t_xnT_tr[(m, g)])
                t_xnT_ev[gi] = t_inc(
                    v.tensor_copy(out=xn1T[:, :].rearrange("p (kc t2) -> p kc t2", t2=T)[:, g * 4:(g + 1) * 4, m * 128:(m + 1) * 128],
                                  in_=ps_xt[stg][:, :].rearrange("p (i t2) -> p i t2", t2=128)), vm)

    for m in range(MT):
        o = (m % 2) * 8

        @blk.sync
        def _(sy, m=m):
            if m >= 2:
                sy.wait_ge(vm.h, t_xn[m - 2])
            t_xload[m] = dma16(sy, SXP, out=x_buf[:, m % 2, :], in_=x_in[m * 128:(m + 1) * 128, :])

        @blk.vector
        def _(v, m=m, o=o):
            wt(v, t_xload[m])
            xb = x_buf[:, m % 2, :]
            v.tensor_scalar(out=dump[:, :], in0=xb, scalar1=1.0, scalar2=0.0, op0=ALU.mult, op1=ALU.add, accum_out=stat[:, o:o + 1])
            v.drain()
            v.tensor_scalar_mul(out=stat[:, o:o + 1], in0=stat[:, o:o + 1], scalar1=1.0 / D)
            v.drain()
            v.scalar_tensor_tensor(out=dump[:, :], in0=xb, scalar=stat[:, o:o + 1], in1=xb, op0=ALU.subtract, op1=ALU.mult, accum_out=stat[:, o + 1:o + 2])
            v.drain()
            t_ln[m] = t_inc(v.tensor_scalar(out=stat[:, o + 2:o + 3], in0=stat[:, o + 1:o + 2], scalar1=1.0 / D, scalar2=EPS_LN, op0=ALU.mult, op1=ALU.add), vm)

        def f_sq(s, m=m, o=o):
            s.wait_ge(vm.h, t_ln[m])
            s.activation(out=stat[:, o + 3:o + 4], in_=stat[:, o + 2:o + 3], func=AF.Ln)
            s.drain()
            s.activation(out=stat[:, o + 4:o + 5], in_=stat[:, o + 3:o + 4], func=AF.Exp, scale=epsc[:, 2:3]).then_inc(am.h, 1)
        t_sq[m] = sadd(f_sq)

        if m >= 1:
            emit_ln1_apply(m - 1)
    emit_ln1_apply(MT - 1)
    t_xn1T_v = t_xnT_ev[MT * 4 - 1]

    KSTAGE = os.environ.get("KSTAGE", "full")
    w1p = weight_prep_make(e1, "w1", w1T, D_w1s, D, DFF, 4096, 0)
    DO_QK = KSTAGE in ("qk", "vu", "full")
    DO_VU = KSTAGE in ("vu", "full")
    DO_WP = (KSTAGE == "full")
    t_wqk, t_qk_mm, t_qk_ev, t_qk_sp = {}, {}, {}, {}

    @blk.gpsimd
    def _(g):
        if not DO_QK:
            return
        for Mc in range(2):
            t_wqk[Mc] = dma16(g, GXP, out=wqk_b[:, Mc % 2, :].rearrange("p (kc mm) -> p kc mm", mm=128),
                                          in_=wqkT[:, Mc * 128:(Mc + 1) * 128].rearrange("(kc p) mm -> p kc mm", p=128))
    if DO_WP:
        for _pf in range(w1p["nslot"]):
            w1p["emit_dma"](_pf)

    for Mc in range(2 * MD if DO_QK else 0):
        @blk.tensor
        def _(t, Mc=Mc):
            if Mc == 0:
                t.wait_ge(vm.h, t_xn1T_v)
            wt(t, t_wqk[Mc])
            for n in range(4):
                if Mc >= 1:
                    t.wait_ge(am.h, t_qk_ev[(Mc - 1, n)])
                for K in range(MD):
                    mmi = t.matmul(ps4[n][:, :], wqk_b[:, Mc % 2, K * 128:(K + 1) * 128],
                                   xn1T[:, K * T + n * 512:K * T + (n + 1) * 512],
                                   start=(K == 0), stop=(K == MD - 1))
                t_qk_mm[(Mc, n)] = t_inc(mmi, tm)

        toks_mm = dict(t_qk_mm)
        tsp_qk = t_qk_sp.get(Mc - 2)

        def f_ev(s, Mc=Mc, toks_mm=toks_mm, tsp_qk=tsp_qk):
            for n in range(4):
                s.wait_ge(tm.h, toks_mm[(Mc, n)])
                if tsp_qk is not None and n == 0:
                    wt(s, tsp_qk)
                s.activation(out=qk_st[:, Mc % 2, n * 512:(n + 1) * 512], in_=ps4[n][:, :], func=AF.Silu).then_inc(am.h, 1)
        sadd(f_ev, ninc=4)
        for n in range(4):
            t_qk_ev[(Mc, n)] = am.n - 3 + n

        @blk.sync
        def _(sy, Mc=Mc):
            sy.wait_ge(am.h, t_qk_ev[(Mc, 3)])
            dst = D_qfm if Mc < MD else D_kfm
            r = (Mc % MD) * 128
            t_qk_sp[Mc] = dma16(sy, SXP, out=dst[r:r + 128, :], in_=qk_st[:, Mc % 2, :])

        if Mc + 2 < 2 * MD:
            @blk.gpsimd
            def _(g, Mc=Mc):
                g.wait_ge(tm.h, t_qk_mm[(Mc, 3)])
                t_wqk[Mc + 2] = dma16(g, GXP, out=wqk_b[:, Mc % 2, :].rearrange("p (kc mm) -> p kc mm", mm=128),
                                                  in_=wqkT[:, (Mc + 2) * 128:(Mc + 3) * 128].rearrange("(kc p) mm -> p kc mm", p=128))
        if DO_WP:
            w1p["emit_chunk"](Mc)

    NVU = 16
    t_wvu, t_vu_mm, t_vu_ev, t_vu_sp = {}, {}, {}, {}

    @blk.gpsimd
    def _(g):
        if not DO_VU:
            return
        for n in range(2):
            t_wvu[n] = dma16(g, GXP, out=wvu_b[:, n % 2, :].rearrange("p (kc j) -> p kc j", j=256),
                                         in_=wvuT[:, n * 256:(n + 1) * 256].rearrange("(kc p) j -> p kc j", p=128))

    for n in range(NVU if DO_VU else 0):
        for m in range(MT):
            @blk.tensor
            def _(t, n=n, m=m):
                if m == 0:
                    wt(t, t_wvu[n])
                idx = n * MT + m
                if idx >= 4:
                    pidx = idx - 4
                    t.wait_ge(vm.h, t_vu_ev[(pidx // MT, pidx % MT)])
                elif idx < 4:
                    t.wait_ge(am.h, t_qk_ev[(2 * MD - 1, m % 4)])
                for K in range(MD):
                    mmi = t.matmul(ps4[m % 4][:, 0:256], xn1T[:, K * T + m * 128:K * T + m * 128 + 128],
                                   wvu_b[:, n % 2, K * 256:(K + 1) * 256],
                                   start=(K == 0), stop=(K == MD - 1))
                t_vu_mm[(n, m)] = t_inc(mmi, tm)

            @blk.vector
            def _(v, n=n, m=m):
                v.wait_ge(tm.h, t_vu_mm[(n, m)])
                idx = n * MT + m
                if idx >= 2:
                    pidx = idx - 2
                    wt(v, t_vu_sp[(pidx // MT, pidx % MT)])
                t_vu_ev[(n, m)] = t_inc(v.tensor_copy(out=vu_st[:, idx % 2, :], in_=ps4[m % 4][:, 0:256]), vm)

            @blk.sync
            def _(sy, n=n, m=m):
                sy.wait_ge(vm.h, t_vu_ev[(n, m)])
                idx = n * MT + m
                if n < 8:
                    dst = D_v[m * 128:(m + 1) * 128, n * 256:(n + 1) * 256]
                else:
                    dst = D_u[m * 128:(m + 1) * 128, (n - 8) * 256:(n - 8 + 1) * 256]
                t_vu_sp[(n, m)] = dma16(sy, SXP, out=dst, in_=vu_st[:, idx % 2, :])

        if n + 2 < NVU:
            @blk.gpsimd
            def _(g, n=n):
                g.wait_ge(tm.h, t_vu_mm[(n, MT - 1)])
                t_wvu[n + 2] = dma16(g, GXP, out=wvu_b[:, n % 2, :].rearrange("p (kc j) -> p kc j", j=256),
                                                 in_=wvuT[:, (n + 2) * 256:(n + 3) * 256].rearrange("(kc p) j -> p kc j", p=128))

    if DO_WP:
        finish_scale(D * DFF, 0, w1p["abs_last"](), w1p["nchunk"], 0, ps_s1[:, :], ps_s128[:, :], stat[:, 5:6])

    barrier()
    if upto <= 1:
        e1.close(); eA.close(); ctx.close()
        return nc
    e1.close()

    e2 = contextlib.ExitStack()
    qT_b = e2.enter_context(sb("qT_b", [128, 2, H * BLK], BF16))
    kT_b = e2.enter_context(sb("kT_b", [128, 2, H * BLK], BF16))
    v_b = e2.enter_context(sb("v_b", [128, 2, 2 * D], BF16))
    u_b = e2.enter_context(sb("u_b", [128, 2, 2 * D], BF16))
    ktm = e2.enter_context(sb("ktm", [128, H * BLK], BF16))
    vp = e2.enter_context(sb("vp", [128, 2 * D], BF16))
    kqs = e2.enter_context(sb("kqs", [128, 2, 2 * BLK], BF16))
    kv = e2.enter_context(sb("kv", [128, H * HD], F32))
    kvb = e2.enter_context(sb("kvb", [128, H * HD], BF16))
    gath = e2.enter_context(sb("gath", [128, 4, H * HD], BF16))
    o_blk = e2.enter_context(sb("o_blk", [128, 2, D], F32))
    qTd = e2.enter_context(sb("qTd", [128, 3, BLK], BF16))
    z_bf = e2.enter_context(sb("z_bf", [128, 2, D], BF16))
    z_ts = e2.enter_context(sb("z_ts", [128, 2, 512], BF16))
    rms_c = e2.enter_context(sb("rms_c", [128, 16], F32))

    ps_tr = [e2.enter_context(nc.psum_tensor(f"ps_tr{i}", [128, 512], BF16)) for i in range(2)]
    ps_kq = [e2.enter_context(nc.psum_tensor(f"ps_kq{i}", [128, 512], F32)) for i in range(2)]
    ps_nddg = [e2.enter_context(nc.psum_tensor(f"ps_nddg{i}", [128, 512], F32)) for i in range(3)]
    ps_kvu = [e2.enter_context(nc.psum_tensor(f"ps_kvu{i}", [128, HD], F32)) for i in range(1)]

    def stream_block(sy, j, need_qu, tokens, extra_wait=None):
        if extra_wait is not None:
            sem_, val_ = extra_wait
            sy.wait_ge(sem_, val_)
        sl = slice(j * BLK, (j + 1) * BLK)
        dma16(sy, SXP, out=kT_b[:, j % 2, :].rearrange("p (hh t) -> p hh t", t=BLK),
                           in_=D_kfm[:, sl].rearrange("(hh p) t -> p hh t", p=128))
        dma16(sy, SXP, out=v_b[:, j % 2, :].rearrange("p (c f) -> p c f", f=D),
                           in_=D_v[sl, :].rearrange("(c p) f -> p c f", p=128))
        if need_qu:
            dma16(sy, SXP, out=qT_b[:, j % 2, :].rearrange("p (hh t) -> p hh t", t=BLK),
                               in_=D_qfm[:, sl].rearrange("(hh p) t -> p hh t", p=128))
            dma16(sy, SXP, out=u_b[:, j % 2, :].rearrange("p (c f) -> p c f", f=D),
                               in_=D_u[sl, :].rearrange("(c p) f -> p c f", p=128))
        tokens[j] = snap(SXP)

    w2p = weight_prep_make(e2, "w2", w2T, D_w2s, D, DFF, 4096, 32, nslot=2)
    for _pf in range(w2p["nslot"]):
        w2p["emit_dma"](_pf)

    t_a_ld, t_a_tr, t_a_ktm, t_a_vp, t_a_mm, t_a_kv = {}, {}, {}, {}, {}, {}

    @blk.vector
    def _(v):
        v.wait_ge(am.h, prep_done)
        t_inc(v.memset(kv[:, :], 0.0), vm)
    t_kv_zero = vm.n

    @blk.sync
    def _(sy):
        for j in range(2):
            stream_block(sy, j, False, t_a_ld)

    for j in range(NBLK):
        for grp in range(8):
            st = grp % 2

            @blk.tensor
            def _(t, j=j, grp=grp, st=st):
                if grp == 0:
                    wt(t, t_a_ld[j])
                prev = t_a_ktm.get((j, grp - 2), t_a_ktm.get((j - 1, 6 + st), 0))
                if prev:
                    t.wait_ge(vm.h, prev)
                for i in range(4):
                    tile = grp * 4 + i
                    h, c = tile // 2, tile % 2
                    mmi = t.transpose(ps_tr[st][:, i * 128:(i + 1) * 128], kT_b[:, j % 2, h * BLK + c * 128:h * BLK + (c + 1) * 128], ident[:, :])
                t_a_tr[(j, grp)] = t_inc(mmi, tm)

            @blk.vector
            def _(v, j=j, grp=grp):
                v.wait_ge(tm.h, t_a_tr[(j, grp)])
                if j > 0:
                    v.wait_ge(tm.h, t_a_mm[(j - 1, 2 * grp + 1)])
                t_a_ktm[(j, grp)] = t_inc(v.tensor_copy(out=ktm[:, grp * 512:(grp + 1) * 512], in_=ps_tr[grp % 2][:, :]), vm)

        @blk.vector
        def _(v, j=j):
            wt(v, t_a_ld[j])
            if j > 0:
                v.wait_ge(tm.h, t_a_mm[(j - 1, H - 1)])
            for c in range(2):
                v.tensor_tensor(out=vp[:, c * D:(c + 1) * D].rearrange("p (hh e) -> p hh e", e=128),
                                in0=v_b[:, j % 2, c * D:(c + 1) * D].rearrange("p (hh e) -> p hh e", e=128),
                                in1=kdec[:, c, :].unsqueeze(2).broadcast_to([128, H, 128]), op=ALU.mult)
            t_a_vp[j] = t_inc(v.engine_nop(), vm)

        for h in range(H):
            st = h % 2

            @blk.tensor
            def _(t, j=j, h=h, st=st):
                if h == 0:
                    t.wait_ge(vm.h, t_a_vp[j])
                prev = t_a_kv.get((j, h - 1), t_a_kv.get((j - 1, H - 1), 0))
                if prev:
                    t.wait_ge(vm.h, prev)
                t.wait_ge(vm.h, t_a_ktm[(j, h // 2)])
                for c in range(2):
                    mmi = t.matmul(ps_kvu[0][:, :], ktm[:, h * BLK + c * 128:h * BLK + (c + 1) * 128],
                                   vp[:, c * D + h * 128:c * D + h * 128 + 128], start=(c == 0), stop=(c == 1))
                t_a_mm[(j, h)] = t_inc(mmi, tm)

            @blk.vector
            def _(v, j=j, h=h, st=st):
                v.wait_ge(tm.h, t_a_mm[(j, h)])
                t_a_kv[(j, h)] = t_inc(
                    v.scalar_tensor_tensor(out=kv[:, h * 128:(h + 1) * 128], in0=kv[:, h * 128:(h + 1) * 128],
                                           scalar=bdec[:, h:h + 1], in1=ps_kvu[0][:, :], op0=ALU.mult, op1=ALU.add), vm)

        if j + 2 < NBLK:
            @blk.sync
            def _(sy, j=j):
                stream_block(sy, j + 2, False, t_a_ld, extra_wait=(tm.h, t_a_mm[(j, H - 1)]))
    t_kvf_done = t_a_kv[(NBLK - 1, H - 1)]

    if os.environ.get("E2STAGE", "full") == "2a0":
        @blk.sync
        def _(sy):
            sy.wait_ge(vm.h, t_kvf_done)
            dma16(sy, SXP, out=D_kv0[:, :], in_=kv[:, :])
        barrier()
        e2.close(); eA.close(); ctx.close()
        return nc

    cc_toks = {}

    @blk.gpsimd
    def _(g):
        g.wait_ge(vm.h, t_kvf_done)
        cc_toks["cci"] = dma16(g, GXP, out=D_cci.ap().rearrange("(hh dd) e -> dd hh e", dd=128),
                               in_=kv[:, :].rearrange("p (hh e) -> p hh e", e=128))
        wt(g, cc_toks["cci"])
        t_inc(g.collective_compute("AllGather", ALU.bypass, replica_groups=[list(range(NC))],
                                   ins=[D_cci.ap().opt()], outs=[D_cco.ap().opt()]), cc, 1)
        g.wait_ge(cc.h, cc.n)

    t_gath, t_sel = {}, {}

    for r in range(NC):
        @blk.gpsimd
        def _(g, r=r):
            if r >= 4:
                g.wait_ge(vm.h, t_sel[r - 4])
            t_gath[r] = dma16(g, GXP, out=gath[:, r % 4, :].rearrange("p (hh e) -> p hh e", e=128),
                                          in_=D_cco.ap()[r * 2048:(r + 1) * 2048, :].rearrange("(hh dd) e -> dd hh e", dd=128))

        @blk.vector
        def _(v, r=r):
            if r == 0:
                v.memset(kvb[:, :], 0.0)
                v.drain()
            wt(v, t_gath[r])
            v.drain()
            t_sel[r] = t_inc(v.scalar_tensor_tensor(out=kvb[:, :], in0=gath[:, r % 4, :], scalar=kvmask_s[:, r:r + 1],
                                                    in1=kvb[:, :], op0=ALU.mult, op1=ALU.add), vm)
    t_kv0 = t_sel[NC - 1]

    @blk.sync
    def _(sy):
        sy.wait_ge(vm.h, t_kv0)
        dma16(sy, SXP, out=D_kv0[:, :], in_=kv[:, :])

    E2STAGE = os.environ.get("E2STAGE", "full")
    if E2STAGE == "2a":
        barrier()
        e2.close(); eA.close(); ctx.close()
        return nc

    t_s_ld, t_s_tr, t_s_ktm, t_s_vp = {}, {}, {}, {}
    t_s_kq, t_s_kqs, t_s_nddg, t_s_kvu, t_s_kvn, t_s_qtd, t_s_oev = {}, {}, {}, {}, {}, {}, {}
    t_s_z, t_s_zsp, t_s_ztr, t_s_zte = {}, {}, {}, {}

    @blk.sync
    def _(sy):
        sy.wait_ge(tm.h, t_a_tr[(NBLK - 2, 7)])
        sy.wait_ge(vm.h, t_a_vp[NBLK - 2])
        stream_block(sy, 0, True, t_s_ld)
        sy.wait_ge(tm.h, t_a_mm[(NBLK - 1, H - 1)])
        stream_block(sy, 1, True, t_s_ld)

    for j in range(NBLK):
        for grp in range(8):
            st = grp % 2

            @blk.tensor
            def _(t, j=j, grp=grp, st=st):
                if j == 0 and grp == 0:
                    t.wait_ge(vm.h, t_kv0)
                if grp == 0:
                    wt(t, t_s_ld[j])
                prev = t_s_ktm.get((j, grp - 2), t_s_ktm.get((j - 1, 6 + st), t_a_ktm[(NBLK - 1, 6 + st)]))
                t.wait_ge(vm.h, prev)
                if grp < 2 and (j - 1, 6 + st) in t_s_zte:
                    t.wait_ge(vm.h, t_s_zte[(j - 1, 6 + st)])
                for i in range(4):
                    tile = grp * 4 + i
                    h, c = tile // 2, tile % 2
                    mmi = t.transpose(ps_tr[st][:, i * 128:(i + 1) * 128], kT_b[:, j % 2, h * BLK + c * 128:h * BLK + (c + 1) * 128], ident[:, :])
                t_s_tr[(j, grp)] = t_inc(mmi, tm)

            @blk.vector
            def _(v, j=j, grp=grp):
                v.wait_ge(tm.h, t_s_tr[(j, grp)])
                kprev = t_s_kvu.get((j - 1, 2 * grp + 1), t_a_mm[(NBLK - 1, 2 * grp + 1)])
                v.wait_ge(tm.h, kprev)
                t_s_ktm[(j, grp)] = t_inc(v.tensor_copy(out=ktm[:, grp * 512:(grp + 1) * 512], in_=ps_tr[grp % 2][:, :]), vm)

        @blk.vector
        def _(v, j=j):
            wt(v, t_s_ld[j])
            vprev = t_s_kvu.get((j - 1, H - 1), t_a_mm[(NBLK - 1, H - 1)])
            v.wait_ge(tm.h, vprev)
            for c in range(2):
                v.tensor_tensor(out=vp[:, c * D:(c + 1) * D].rearrange("p (hh e) -> p hh e", e=128),
                                in0=v_b[:, j % 2, c * D:(c + 1) * D].rearrange("p (hh e) -> p hh e", e=128),
                                in1=kdec[:, c, :].unsqueeze(2).broadcast_to([128, H, 128]), op=ALU.mult)
            pq0 = t_s_nddg.get((j - 1, H - 3), 0)
            if pq0:
                v.wait_ge(tm.h, pq0)
            t_s_qtd[(j, 0)] = t_inc(v.tensor_tensor(out=qTd[:, 0, :], in0=qT_b[:, j % 2, 0:BLK],
                                                    in1=qdecT[:, 0, :], op=ALU.mult), vm)
            t_s_vp[j] = t_inc(v.engine_nop(), vm)

        def emit_diag(h2, j=j):
            st2 = h2 % 2
            sd32 = h2 % 3

            @blk.tensor
            def _(t, j=j, h2=h2, st2=st2, sd32=sd32):
                if h2 == 0:
                    t.wait_ge(vm.h, t_s_vp[j])
                po = t_s_oev.get((j, h2 - 3), t_s_oev.get((j - 1, H + h2 - 3), 0))
                if po:
                    t.wait_ge(am.h, po)
                pn = t_s_kvn.get((j - 1, h2), t_kv0)
                t.wait_ge(vm.h, pn)
                t.wait_ge(vm.h, t_s_qtd[(j, h2)])
                t.wait_ge(vm.h, t_s_kqs[(j, h2)])
                t.matmul(ps_nddg[sd32][:, 256:384], qTd[:, sd32, 0:128], kvb[:, h2 * 128:(h2 + 1) * 128], start=True, stop=False)
                t.matmul(ps_nddg[sd32][:, 256:384], kqs[:, st2, 0:128], v_b[:, j % 2, h2 * 128:h2 * 128 + 128], start=False, stop=True)
                t.matmul(ps_nddg[sd32][:, 384:512], qTd[:, sd32, 128:256], kvb[:, h2 * 128:(h2 + 1) * 128], start=True, stop=False)
                t.matmul(ps_nddg[sd32][:, 384:512], kqs[:, st2, 128:256], v_b[:, j % 2, h2 * 128:h2 * 128 + 128], start=False, stop=False)
                mmi = t.matmul(ps_nddg[sd32][:, 384:512], kqs[:, st2, BLK + 128:BLK + 256], v_b[:, j % 2, D + h2 * 128:D + h2 * 128 + 128], start=False, stop=True)
                t_s_nddg[(j, h2)] = t_inc(mmi, tm)
                pkv = t_s_kvn.get((j, h2 - 1), t_s_kvn.get((j - 1, H - 1), 0))
                if pkv:
                    t.wait_ge(vm.h, pkv)
                t.wait_ge(vm.h, t_s_ktm[(j, h2 // 2)])
                for c in range(2):
                    mmi = t.matmul(ps_kvu[0][:, :], ktm[:, h2 * BLK + c * 128:h2 * BLK + (c + 1) * 128],
                                   vp[:, c * D + h2 * 128:c * D + h2 * 128 + 128], start=(c == 0), stop=(c == 1))
                t_s_kvu[(j, h2)] = t_inc(mmi, tm)

        def emit_o(h2, j=j):
            tnd = t_s_nddg[(j, h2)]
            sd32 = h2 % 3

            def f_oev(s, j=j, h2=h2, sd32=sd32, tnd=tnd):
                s.wait_ge(tm.h, tnd)
                if j > 0 and h2 <= 1:
                    wt(s, t_s_zsp[j - 1])
                for cq in range(2):
                    mmi = s.activation(out=o_blk[:, cq, h2 * 128:(h2 + 1) * 128],
                                       in_=ps_nddg[sd32][:, 256 + cq * 128:256 + (cq + 1) * 128], func=AF.Copy)
                mmi.then_inc(am.h, 1)
            t_s_oev[(j, h2)] = sadd(f_oev)

        for h in range(H):
            st = h % 2

            @blk.tensor
            def _(t, j=j, h=h, st=st):
                pk = t_s_kqs.get((j, h - 2), t_s_kqs.get((j - 1, H - 2 + st), 0))
                if pk:
                    t.wait_ge(vm.h, pk)
                for kc in range(2):
                    mmi = t.matmul(ps_kq[st][:, kc * BLK:(kc + 1) * BLK],
                                   kT_b[:, j % 2, h * BLK + kc * 128:h * BLK + (kc + 1) * 128],
                                   qT_b[:, j % 2, h * BLK:(h + 1) * BLK], start=True, stop=True)
                t_s_kq[(j, h)] = t_inc(mmi, tm)

            if h >= 1:
                emit_diag(h - 1)

            @blk.vector
            def _(v, j=j, h=h, st=st):
                v.wait_ge(tm.h, t_s_kq[(j, h)])
                if h + 1 < H:
                    t_s_qtd[(j, h + 1)] = t_inc(
                        v.tensor_tensor(out=qTd[:, (h + 1) % 3, :], in0=qT_b[:, j % 2, (h + 1) * BLK:(h + 2) * BLK],
                                        in1=qdecT[:, h + 1, :], op=ALU.mult), vm)
                pd = t_s_nddg.get((j, h - 2), t_s_nddg.get((j - 1, H - 2 + st), 0))
                if pd:
                    v.wait_ge(tm.h, pd)
                v.tensor_tensor(out=kqs[:, st, :], in0=ps_kq[st][:, :],
                                in1=decayT[:, h, :, :].rearrange("p a b -> p (a b)"), op=ALU.mult)
                t_s_kqs[(j, h)] = t_inc(v.engine_nop(), vm)
                if h >= 1:
                    v.wait_ge(tm.h, t_s_kvu[(j, h - 1)])
                    t_s_kvn[(j, h - 1)] = t_inc(
                        v.scalar_tensor_tensor(out=kvb[:, (h - 1) * 128:h * 128], in0=kvb[:, (h - 1) * 128:h * 128],
                                               scalar=bdec[:, h - 1:h], in1=ps_kvu[0][:, :], op0=ALU.mult, op1=ALU.add), vm)

            if h >= 1:
                emit_o(h - 1)
            if h % 4 == 0:
                w2p["emit_chunk"](j * 4 + h // 4)
        emit_diag(H - 1)

        @blk.vector
        def _(v, j=j):
            v.wait_ge(tm.h, t_s_kvu[(j, H - 1)])
            t_s_kvn[(j, H - 1)] = t_inc(
                v.scalar_tensor_tensor(out=kvb[:, (H - 1) * 128:H * 128], in0=kvb[:, (H - 1) * 128:H * 128],
                                       scalar=bdec[:, H - 1:H], in1=ps_kvu[0][:, :], op0=ALU.mult, op1=ALU.add), vm)

        emit_o(H - 1)

        for cq in range(2):
            o = cq * 6

            @blk.vector
            def _(v, j=j, cq=cq, o=o):
                if cq == 0:
                    v.wait_ge(am.h, t_s_oev[(j, H - 1)])
                v.drain()
                v.scalar_tensor_tensor(out=gath[:, 0, :], in0=o_blk[:, cq, :], scalar=0.0, in1=o_blk[:, cq, :],
                                       op0=ALU.subtract, op1=ALU.mult, accum_out=rms_c[:, o:o + 1])
                v.drain()
                t_inc(v.tensor_scalar(out=rms_c[:, o + 1:o + 2], in0=rms_c[:, o:o + 1], scalar1=1.0 / D, scalar2=EPS_RMS, op0=ALU.mult, op1=ALU.add), vm)
            t_rms1 = vm.n

            def f_rms(s, o=o, t_rms1=t_rms1):
                s.wait_ge(vm.h, t_rms1)
                s.activation(out=rms_c[:, o + 2:o + 3], in_=rms_c[:, o + 1:o + 2], func=AF.Ln)
                s.drain()
                s.activation(out=rms_c[:, o + 3:o + 4], in_=rms_c[:, o + 2:o + 3], func=AF.Exp, scale=epsc[:, 2:3]).then_inc(am.h, 1)
            t_rms2 = sadd(f_rms)

            @blk.vector
            def _(v, j=j, cq=cq, o=o, t_rms2=t_rms2):
                v.wait_ge(am.h, t_rms2)
                v.drain()
                if j > 0 and cq == 0:
                    v.wait_ge(tm.h, t_s_ztr[(j - 1, 7)])
                t_s_z[(j, cq)] = t_inc(
                    v.scalar_tensor_tensor(out=z_bf[:, cq, :], in0=o_blk[:, cq, :], scalar=rms_c[:, o + 3:o + 4],
                                           in1=u_b[:, j % 2, cq * D:(cq + 1) * D], op0=ALU.mult, op1=ALU.mult), vm)
        t_s_z[j] = t_s_z[(j, 1)]

        for c2 in range(2):
            for g2 in range(4):
                gi2 = c2 * 4 + g2

                @blk.tensor
                def _(t, j=j, c2=c2, g2=g2, gi2=gi2):
                    st2_ = gi2 % 2
                    if gi2 == 0:
                        t.wait_ge(vm.h, t_s_z[j])
                    prev = t_s_zte.get((j, gi2 - 2), t_s_ktm[(j, 6 + st2_)])
                    t.wait_ge(vm.h, prev)
                    for i in range(4):
                        K = g2 * 4 + i
                        mmi = t.transpose(ps_tr[st2_][:, i * 128:(i + 1) * 128], z_bf[:, c2, K * 128:(K + 1) * 128], ident[:, :])
                    t_s_ztr[(j, gi2)] = t_inc(mmi, tm)

                @blk.vector
                def _(v, j=j, c2=c2, g2=g2, gi2=gi2):
                    st2_ = gi2 % 2
                    v.wait_ge(tm.h, t_s_ztr[(j, gi2)])
                    tokp = t_s_zsp.get((j, gi2 - 2), t_s_zsp.get((j - 1, 6 + st2_)))
                    if tokp is not None:
                        wt(v, tokp)
                    t_s_zte[(j, gi2)] = t_inc(
                        v.tensor_copy(out=z_ts[:, gi2 % 2, :], in_=ps_tr[st2_][:, :]), vm)

                @blk.sync
                def _(sy, j=j, c2=c2, g2=g2, gi2=gi2):
                    sy.wait_ge(vm.h, t_s_zte[(j, gi2)])
                    t_s_zsp[(j, gi2)] = dma16(sy, SXP,
                        out=D_zT[g2 * 512:(g2 + 1) * 512, j * BLK + c2 * 128:j * BLK + (c2 + 1) * 128].rearrange("(i p) t2 -> p i t2", p=128),
                        in_=z_ts[:, gi2 % 2, :].rearrange("p (i t2) -> p i t2", t2=128))
        t_s_zsp[j] = t_s_zsp[(j, 7)]

        if j + 2 < NBLK:
            @blk.sync
            def _(sy, j=j):
                sy.wait_ge(tm.h, t_s_nddg[(j, H - 1)])
                sy.wait_ge(vm.h, t_s_z[j])
                stream_block(sy, j + 2, True, t_s_ld)

    finish_scale(D * DFF, 1, w2p["abs_last"](), w2p["nchunk"], 32,
                 ps_kq[0][0:1, 0:1], ps_kq[1][:, 0:1], rms_c[:, 15:16],
                 pe_wait_vm=t_s_kvn[(NBLK - 1, H - 1)])

    barrier()
    if upto <= 2:
        e2.close(); eA.close(); ctx.close()
        return nc
    e2.close()
    eA.close()

    e2c = contextlib.ExitStack()
    wo_b = e2c.enter_context(sb("wo_b", [128, MD * D], BF16))
    zT_b = e2c.enter_context(sb("zT_b", [128, 2, MD * 512], BF16))
    x_b2 = e2c.enter_context(sb("x_b2", [128, 2, D], F32))
    at_sb = e2c.enter_context(sb("at_sb", [128, 2, D], F32))
    x2_sb = e2c.enter_context(sb("x2_sb", [128, 2, D], F32))
    y_sb = e2c.enter_context(sb("y_sb", [128, 2, D], F32))
    qy_bf = e2c.enter_context(sb("qy_bf", [128, 2, D], BF16))
    st2 = e2c.enter_context(sb("st2", [128, 48], F32))
    gq = e2c.enter_context(sb("gq", [128, 96], F32))
    dump3 = e2c.enter_context(sb("dump3", [128, 2, D], F32))
    dumpA = e2c.enter_context(sb("dumpA", [128, 2, D], F32))
    gq2 = e2c.enter_context(sb("gq2", [128, 2, 96], F32))

    ps4c = [e2c.enter_context(nc.psum_tensor(f"ps_e2c_{i}", [128, 512], F32)) for i in range(4)]
    ps_qyt = [e2c.enter_context(nc.psum_tensor(f"ps_qyt{i}", [128, 512], BF16)) for i in range(2)]
    qyt_st = e2c.enter_context(sb("qyt_st", [128, 2, 512], BF16))

    t_wo = {}

    @blk.gpsimd
    def _(g):
        for q in range(4):
            t_wo[q] = dma16(g, GXP, out=wo_b[:, :].rearrange("p (kc n2) -> p kc n2", n2=D)[:, 4 * q:4 * (q + 1), :],
                                        in_=woT[:, :].rearrange("(kc p) n2 -> p kc n2", p=128)[:, 4 * q:4 * (q + 1), :])
        t_wo["all"] = snap(GXP)

    t_zT, t_x2ld = {}, {}
    t_wo_mm, t_wo_ev, t_x2c = {}, {}, {}
    t_ln2, t_ln2b, t_ln3s, t_ln3a, t_ln3b = {}, {}, {}, {}, {}
    t_qy, t_qysp, t_x2sp = {}, {}, {}
    t_qyt_tr, t_qyt_ev, t_qyt_sp = {}, {}, {}

    def emit_qyT(m2):
        for g2 in range(4):
            gi2 = m2 * 4 + g2

            @blk.tensor
            def _(t, m2=m2, g2=g2, gi2=gi2):
                if g2 == 0:
                    t.wait_ge(vm.h, t_qy[m2])
                prev = t_qyt_ev.get(gi2 - 2, 0)
                if prev:
                    t.wait_ge(vm.h, prev)
                for i in range(4):
                    K = g2 * 4 + i
                    mmi = t.transpose(ps_qyt[gi2 % 2][:, i * 128:(i + 1) * 128], qy_bf[:, m2 % 2, K * 128:(K + 1) * 128], ident[:, :])
                t_qyt_tr[(m2, g2)] = t_inc(mmi, tm)

            @blk.vector
            def _(v, m2=m2, g2=g2, gi2=gi2):
                v.wait_ge(tm.h, t_qyt_tr[(m2, g2)])
                tokp = t_qyt_sp.get(gi2 - 2)
                if tokp is not None:
                    wt(v, tokp)
                t_qyt_ev[gi2] = t_inc(v.tensor_copy(out=qyt_st[:, gi2 % 2, :], in_=ps_qyt[gi2 % 2][:, :]), vm)

            @blk.sync
            def _(sy, m2=m2, g2=g2, gi2=gi2):
                sy.wait_ge(vm.h, t_qyt_ev[gi2])
                t_qyt_sp[gi2] = dma16(sy, SXP,
                    out=D_qayT[g2 * 512:(g2 + 1) * 512, m2 * 128:(m2 + 1) * 128].rearrange("(i p) t2 -> p i t2", p=128),
                    in_=qyt_st[:, gi2 % 2, :].rearrange("p (i t2) -> p i t2", t2=128))

    def s_mm(m):
        gi, mi = m // 4, m % 4
        if mi == 0:
            @blk.gpsimd
            def _(g_, gi=gi):
                if gi >= 2:
                    g_.wait_ge(tm.h, t_wo_mm[(gi - 2, 3, 3)])
                t_zT[gi] = dma16(g_, GXP,
                    out=zT_b[:, gi % 2, :].rearrange("p (kc t2) -> p kc t2", t2=512),
                    in_=D_zT[:, gi * 512:(gi + 1) * 512].rearrange("(kc p) t2 -> p kc t2", p=128))

        @blk.sync
        def _(sy, m=m):
            if m >= 2:
                sy.wait_ge(vm.h, t_x2c.get(m - 2, 0))
            t_x2ld[m] = dma16(sy, SXP, out=x_b2[:, m % 2, :], in_=x_in[m * 128:(m + 1) * 128, :])

        @blk.tensor
        def _(t, gi=gi, mi=mi, m=m):
            if mi == 0:
                wt(t, t_zT[gi])
                if gi == 0:
                    wt(t, t_wo["all"])
            for n in range(4):
                if m >= 1 and n == 0:
                    t.wait_ge(am.h, t_wo_ev[m - 1])
                for K in range(MD):
                    mmi = t.matmul(ps4c[n][:, :], zT_b[:, gi % 2, K * 512 + mi * 128:K * 512 + mi * 128 + 128],
                                   wo_b[:, K * D + n * 512:K * D + (n + 1) * 512],
                                   start=(K == 0), stop=(K == MD - 1))
                t_wo_mm[(gi, mi, n)] = t_inc(mmi, tm)

        o = (m % 2) * 24
        toks_wo = dict(t_wo_mm)

        tx2p = t_x2c.get(m - 2, 0)

        def f_ev2(s, gi=gi, mi=mi, m=m, o=o, toks_wo=toks_wo, tx2p=tx2p):
            if tx2p:
                s.wait_ge(vm.h, tx2p)
            for n in range(4):
                s.wait_ge(tm.h, toks_wo[(gi, mi, n)])
                s.activation(out=at_sb[:, m % 2, n * 512:(n + 1) * 512], in_=ps4c[n][:, :], func=AF.Copy,
                             accum_out=st2[:, o + n:o + n + 1])
                mmi = s.activation(out=dumpA[:, m % 2, n * 512:(n + 1) * 512], in_=ps4c[n][:, :], func=AF.Square,
                                   accum_out=st2[:, o + 4 + n:o + 5 + n])
            mmi.then_inc(am.h, 1)
        t_wo_ev[m] = sadd(f_ev2)

    def s_stats2(m):
        o = (m % 2) * 24

        @blk.vector
        def _(v, m=m, o=o):
            v.wait_ge(am.h, t_wo_ev[m])
            v.drain()
            v.tensor_reduce(out=st2[:, o + 8:o + 9], in_=st2[:, o:o + 4], axis=AX.X, op=ALU.add)
            v.tensor_reduce(out=st2[:, o + 9:o + 10], in_=st2[:, o + 4:o + 8], axis=AX.X, op=ALU.add)
            v.drain()
            v.tensor_scalar_mul(out=st2[:, o + 8:o + 9], in0=st2[:, o + 8:o + 9], scalar1=1.0 / D)
            v.drain()
            v.tensor_tensor(out=st2[:, o + 10:o + 11], in0=st2[:, o + 8:o + 9], in1=st2[:, o + 8:o + 9], op=ALU.mult)
            v.drain()
            t_ln2[m] = t_inc(v.tensor_scalar(out=st2[:, o + 9:o + 10], in0=st2[:, o + 9:o + 10], scalar1=1.0 / D, scalar2=st2[:, o + 10:o + 11], op0=ALU.mult, op1=ALU.subtract), vm)

        def f_ln2(s, m=m, o=o, tk=t_ln2[m]):
            s.wait_ge(vm.h, tk)
            s.activation(out=st2[:, o + 11:o + 12], in_=st2[:, o + 9:o + 10], func=AF.Ln, bias=epsc[:, 0:1])
            s.drain()
            s.activation(out=st2[:, o + 12:o + 13], in_=st2[:, o + 11:o + 12], func=AF.Exp, scale=epsc[:, 2:3]).then_inc(am.h, 1)
        t_ln2b[m] = sadd(f_ln2)

    def s_apply2(m):
        o = (m % 2) * 24

        @blk.vector
        def _(v, m=m, o=o):
            v.wait_ge(am.h, t_ln2b[m])
            wt(v, t_x2ld[m])
            if m >= 2:
                wt(v, t_x2sp[m - 2])
            v.drain()
            v.tensor_scalar(out=dump3[:, m % 2, :], in0=at_sb[:, m % 2, :], scalar1=st2[:, o + 8:o + 9], scalar2=st2[:, o + 12:o + 13], op0=ALU.subtract, op1=ALU.mult)
            v.drain()
            t_x2c[m] = t_inc(v.tensor_tensor(out=x2_sb[:, m % 2, :], in0=dump3[:, m % 2, :], in1=x_b2[:, m % 2, :], op=ALU.add), vm)

        def f_x2s(s, m=m, o=o, tk=t_x2c[m]):
            s.wait_ge(vm.h, tk)
            s.activation(out=dumpA[:, m % 2, :], in_=x2_sb[:, m % 2, :], func=AF.Copy, accum_out=st2[:, o + 14:o + 15])
            s.drain()
            s.activation(out=dumpA[:, m % 2, :], in_=x2_sb[:, m % 2, :], func=AF.Square, accum_out=st2[:, o + 15:o + 16]).then_inc(am.h, 1)
        t_ln3s[m] = sadd(f_x2s)

    def s_stats3(m):
        o = (m % 2) * 24

        @blk.vector
        def _(v, m=m, o=o):
            v.wait_ge(am.h, t_ln3s[m])
            v.drain()
            v.tensor_scalar_mul(out=st2[:, o + 16:o + 17], in0=st2[:, o + 14:o + 15], scalar1=1.0 / D)
            v.drain()
            v.tensor_tensor(out=st2[:, o + 17:o + 18], in0=st2[:, o + 16:o + 17], in1=st2[:, o + 16:o + 17], op=ALU.mult)
            v.drain()
            t_ln3a[m] = t_inc(v.tensor_scalar(out=st2[:, o + 15:o + 16], in0=st2[:, o + 15:o + 16], scalar1=1.0 / D, scalar2=st2[:, o + 17:o + 18], op0=ALU.mult, op1=ALU.subtract), vm)

        def f_ln3(s, m=m, o=o, tk=t_ln3a[m]):
            s.wait_ge(vm.h, tk)
            s.activation(out=st2[:, o + 18:o + 19], in_=st2[:, o + 15:o + 16], func=AF.Ln, bias=epsc[:, 0:1])
            s.drain()
            s.activation(out=st2[:, o + 19:o + 20], in_=st2[:, o + 18:o + 19], func=AF.Exp, scale=epsc[:, 2:3]).then_inc(am.h, 1)
        t_ln3b[m] = sadd(f_ln3)

    def s_qa(m):
        o = (m % 2) * 24

        @blk.vector
        def _(v, m=m, o=o):
            v.wait_ge(am.h, t_ln3b[m])
            v.drain()
            if m >= 2:
                v.wait_ge(tm.h, t_qyt_tr.get((m - 2, 3), 0))
            b2 = m % 2
            g_ = gq2[:, b2, :]
            v.tensor_scalar(out=y_sb[:, b2, :], in0=x2_sb[:, b2, :], scalar1=st2[:, o + 16:o + 17], scalar2=st2[:, o + 19:o + 20], op0=ALU.subtract, op1=ALU.mult)
            v.drain()
            v.tensor_reduce(out=g_[:, 0:32], in_=y_sb[:, b2, :].rearrange("p (gg k) -> p gg k", k=64), axis=AX.X, op=ALU.max, apply_absolute_value=True)
            v.drain()
            v.tensor_scalar_max(out=g_[:, 0:32], in0=g_[:, 0:32], scalar1=EPS_QA)
            v.drain()
            v.reciprocal(out=g_[:, 32:64], in_=g_[:, 0:32])
            v.drain()
            v.tensor_scalar_mul(out=g_[:, 64:96], in0=g_[:, 0:32], scalar1=1.0 / 127.0)
            v.scalar_tensor_tensor(out=dump3[:, b2, :].rearrange("p (gg k) -> p gg k", k=64), in0=y_sb[:, b2, :].rearrange("p (gg k) -> p gg k", k=64),
                                   scalar=127.0, in1=g_[:, 32:64].unsqueeze(2).broadcast_to([128, 32, 64]), op0=ALU.mult, op1=ALU.mult)
            v.drain()
            v.tensor_scalar(out=dump3[:, b2, :], in0=dump3[:, b2, :], scalar1=MAGIC, scalar2=MAGIC, op0=ALU.add, op1=ALU.subtract)
            v.drain()
            t_qy[m] = t_inc(v.tensor_tensor(out=qy_bf[:, b2, :].rearrange("p (gg k) -> p gg k", k=64),
                                            in0=dump3[:, b2, :].rearrange("p (gg k) -> p gg k", k=64),
                                            in1=g_[:, 64:96].unsqueeze(2).broadcast_to([128, 32, 64]), op=ALU.mult), vm)

        @blk.sync
        def _(sy, m=m):
            sy.wait_ge(vm.h, t_qy[m])
            t_x2sp[m] = dma16(sy, SXP, out=D_x2[m * 128:(m + 1) * 128, :], in_=x2_sb[:, m % 2, :])
        if m >= 1:
            emit_qyT(m - 1)

    stages = [s_qa, s_stats3, s_apply2, s_stats2, s_mm]
    for k in range(MT + len(stages) - 1):
        for lag, st_fn in enumerate(stages):
            mm_ = k - (len(stages) - 1 - lag)
            if 0 <= mm_ < MT:
                st_fn(mm_)
    emit_qyT(MT - 1)


    barrier()
    if upto <= 3:
        e2c.close(); ctx.close()
        return nc
    e2c.close()

    e3 = contextlib.ExitStack()
    qyT = e3.enter_context(sb("qyT", [128, MD * T], BF16))
    w12_b = e3.enter_context(sb("w12_b", [128, 2, 2, MD * 512], BF16))
    silu_sb = e3.enter_context(sb("silu_sb", [128, 2, 512], F32))
    h_sb = e3.enter_context(sb("h_sb", [128, 2, 512], F32))
    qh_bf = e3.enter_context(sb("qh_bf", [128, 3, 512], BF16))
    gq3 = e3.enter_context(sb("gq3", [128, 2, 24], F32))
    tq3 = e3.enter_context(sb("tq3", [128, 2, 512], F32))

    ps3 = [e3.enter_context(nc.psum_tensor(f"ps_e3_{i}", [128, 512], F32)) for i in range(4)]
    ps_t3 = [e3.enter_context(nc.psum_tensor(f"ps_t3_{i}", [128, 512], BF16)) for i in range(2)]
    t3_st = e3.enter_context(sb("t3_st", [128, 2, 512], BF16))
    w3tmp = e3.enter_context(sb("w3tmp", [128, 1], F32))

    t_qyT_d = {}

    @blk.gpsimd
    def _(g_):
        t_qyT_d["tok"] = dma16(g_, GXP, out=qyT[:, :].rearrange("p (kc t2) -> p kc t2", t2=T),
                               in_=D_qayT[:, :].rearrange("(kc p) t2 -> p kc t2", p=128))
    t_qyT = t_qyT_d["tok"]

    NSLAB = 16
    w3p = weight_prep_make(e3, "w3", w3T, D_w3s, DFF, D, 2048, 0)
    for _pf in range(w3p["nslot"]):
        w3p["emit_dma"](_pf)
    t_w12, t_3mm, t_3sl, t_3h, t_3qh, t_3sp, t_3tr, t_3te = {}, {}, {}, {}, {}, {}, {}, {}

    @blk.sync
    def _(sy):
        for sl in range(2):
            for w in range(2):
                src = [D_w1s, D_w2s][w]
                dma16(sy, SXP, out=w12_b[:, sl % 2, w, :].rearrange("p (kc n2) -> p kc n2", n2=512),
                                   in_=src[:, sl * 512:(sl + 1) * 512].rearrange("(kc p) n2 -> p kc n2", p=128))
            t_w12[sl] = snap(SXP)

    for sl in range(NSLAB):
        for m in range(MT):
            idx = sl * MT + m
            bs = (idx % 2) * 2
            b = idx % 2

            @blk.tensor
            def _(t, sl=sl, m=m, idx=idx, bs=bs):
                if m == 0:
                    wt(t, t_w12[sl])
                    if sl == 0:
                        wt(t, t_qyT)
                if idx >= 2:
                    pidx = idx - 2
                    t.wait_ge(vm.h, t_3h[pidx])
                for w in range(2):
                    for K in range(MD):
                        mmi = t.matmul(ps3[bs + w][:, :], qyT[:, K * T + m * 128:K * T + m * 128 + 128],
                                       w12_b[:, sl % 2, w, K * 512:(K + 1) * 512],
                                       start=(K == 0), stop=(K == MD - 1))
                t_3mm[idx] = t_inc(mmi, tm)

            def f_sl(s, idx=idx, bs=bs, b=b, tmm=t_3mm[idx], th=t_3h.get(idx - 2)):
                s.wait_ge(tm.h, tmm)
                if th:
                    s.wait_ge(vm.h, th)
                s.activation(out=silu_sb[:, b, :], in_=ps3[bs][:, :], func=AF.Silu, scale=wscale[:, 0:1]).then_inc(am.h, 1)
            t_3sl[idx] = sadd(f_sl)

            @blk.vector
            def _(v, idx=idx, bs=bs, b=b, m=m, sl=sl):
                v.wait_ge(am.h, t_3sl[idx])
                v.drain()
                if idx >= 3:
                    v.wait_ge(tm.h, t_3tr[idx - 3])
                v.scalar_tensor_tensor(out=h_sb[:, b, :], in0=ps3[bs + 1][:, :], scalar=wscale[:, 1:2], in1=silu_sb[:, b, :], op0=ALU.mult, op1=ALU.mult)
                v.drain()
                t_3h[idx] = t_inc(v.engine_nop(), vm)
                v.tensor_reduce(out=gq3[:, b, 0:8], in_=h_sb[:, b, :].rearrange("p (gg k) -> p gg k", k=64), axis=AX.X, op=ALU.max, apply_absolute_value=True)
                v.drain()
                v.tensor_scalar_max(out=gq3[:, b, 0:8], in0=gq3[:, b, 0:8], scalar1=EPS_QA)
                v.drain()
                v.reciprocal(out=gq3[:, b, 8:16], in_=gq3[:, b, 0:8])
                v.drain()
                v.tensor_scalar_mul(out=gq3[:, b, 16:24], in0=gq3[:, b, 0:8], scalar1=1.0 / 127.0)
                v.scalar_tensor_tensor(out=tq3[:, b, :].rearrange("p (gg k) -> p gg k", k=64), in0=h_sb[:, b, :].rearrange("p (gg k) -> p gg k", k=64),
                                       scalar=127.0, in1=gq3[:, b, 8:16].unsqueeze(2).broadcast_to([128, 8, 64]), op0=ALU.mult, op1=ALU.mult)
                v.drain()
                v.tensor_scalar(out=tq3[:, b, :], in0=tq3[:, b, :], scalar1=MAGIC, scalar2=MAGIC, op0=ALU.add, op1=ALU.subtract)
                v.drain()
                t_3qh[idx] = t_inc(v.tensor_tensor(out=qh_bf[:, idx % 3, :].rearrange("p (gg k) -> p gg k", k=64),
                                                   in0=tq3[:, b, :].rearrange("p (gg k) -> p gg k", k=64),
                                                   in1=gq3[:, b, 16:24].unsqueeze(2).broadcast_to([128, 8, 64]), op=ALU.mult), vm)

            def emit_qh_tr(idx2):
                if idx2 < 0 or idx2 >= NSLAB * MT:
                    return
                sl2, m2 = idx2 // MT, idx2 % MT
                s3_ = idx2 % 3
                b2 = idx2 % 2

                @blk.tensor
                def _(t, idx2=idx2, b2=b2, s3_=s3_):
                    t.wait_ge(vm.h, t_3qh[idx2])
                    prev = t_3te.get(idx2 - 2, 0)
                    if prev:
                        t.wait_ge(vm.h, prev)
                    for i in range(4):
                        mmi = t.transpose(ps_t3[b2][:, i * 128:(i + 1) * 128], qh_bf[:, s3_, i * 128:(i + 1) * 128], ident[:, :])
                    t_3tr[idx2] = t_inc(mmi, tm)

                @blk.vector
                def _(v, idx2=idx2, b2=b2):
                    v.wait_ge(tm.h, t_3tr[idx2])
                    if idx2 >= 2:
                        wt(v, t_3sp[idx2 - 2])
                    t_3te[idx2] = t_inc(v.tensor_copy(out=t3_st[:, b2, :], in_=ps_t3[b2][:, :]), vm)

                @blk.sync
                def _(sy, idx2=idx2, sl2=sl2, m2=m2, b2=b2):
                    sy.wait_ge(vm.h, t_3te[idx2])
                    t_3sp[idx2] = dma16(sy, SXP,
                        out=D_qahT[sl2 * 512:(sl2 + 1) * 512, m2 * 128:(m2 + 1) * 128].rearrange("(i p) t2 -> p i t2", p=128),
                        in_=t3_st[:, b2, :].rearrange("p (i t2) -> p i t2", t2=128))
            emit_qh_tr(idx - 2)
            if idx == NSLAB * MT - 1:
                emit_qh_tr(idx - 1)
                emit_qh_tr(idx)

            if idx % 4 == 0:
                w3p["emit_chunk"](idx // 4)

        if sl + 2 < NSLAB:
            @blk.sync
            def _(sy, sl=sl):
                sy.wait_ge(tm.h, t_3mm[sl * MT + MT - 1])
                for w in range(2):
                    src = [D_w1s, D_w2s][w]
                    dma16(sy, SXP, out=w12_b[:, sl % 2, w, :].rearrange("p (kc n2) -> p kc n2", n2=512),
                                       in_=src[:, (sl + 2) * 512:(sl + 3) * 512].rearrange("(kc p) n2 -> p kc n2", p=128))
                t_w12[sl + 2] = snap(SXP)

    finish_scale(DFF * D, 2, w3p["abs_last"](), w3p["nchunk"], 0, ps3[0][0:1, 0:1], ps3[1][:, 0:1], w3tmp[:, :],
                 pe_wait_vm=t_3h[NSLAB * MT - 1], pe_wait_am=t_3sl[NSLAB * MT - 1])

    barrier()
    if upto <= 4:
        e3.close(); ctx.close()
        return nc
    e3.close()

    e4 = contextlib.ExitStack()
    w3q = e4.enter_context(sb("w3q", [128, 2, MF * 512], BF16))
    qhT = e4.enter_context(sb("qhT", [128, 2, MF * 256], BF16))
    pre4 = e4.enter_context(sb("pre4", [128, 2, 512], BF16))
    ps_a = [e4.enter_context(nc.psum_tensor(f"ps_e4_{i}", [128, 512], F32)) for i in range(4)]
    ps_dmp = e4.enter_context(nc.psum_tensor("ps_dmp", [128, 512], F32))

    t_w3q, t_qhT, t_4mm, t_4ev, t_4sp = {}, {}, {}, {}, {}

    for q in range(4):
        @blk.gpsimd
        def _(g_, q=q):
            if q >= 2:
                g_.wait_ge(tm.h, t_4mm[(q - 2, 15)])
            t_w3q[q] = dma16(g_, GXP, out=w3q[:, q % 2, :].rearrange("p (kc n2) -> p kc n2", n2=512),
                             in_=D_w3s[:, q * 512:(q + 1) * 512].rearrange("(kc p) n2 -> p kc n2", p=128))

        def emit_qhT_load(g2, q=q):
            @blk.gpsimd
            def _(g_, q=q, g2=g2):
                pg = t_4mm.get((q, (g2 - 2) * 2 + 1), t_4mm.get((q - 1, (6 + g2 % 2) * 2 + 1), None))
                if pg is not None:
                    g_.wait_ge(tm.h, pg)
                t_qhT[(q, g2)] = dma16(g_, GXP,
                    out=qhT[:, g2 % 2, :].rearrange("p (kc t2) -> p kc t2", t2=256),
                    in_=D_qahT[:, g2 * 256:(g2 + 1) * 256].rearrange("(kc p) t2 -> p kc t2", p=128))
        emit_qhT_load(0)
        emit_qhT_load(1)

        for g in range(8):
            for mi in range(2):
                idx = g * 2 + mi
                m = idx
                bank = idx % 4
                b = idx % 2
                csum = 8 * m + q
                csq = 8 * m + 4 + q

                @blk.tensor
                def _(t, q=q, g=g, mi=mi, idx=idx, bank=bank):
                    if mi == 0:
                        wt(t, t_qhT[(q, g)])
                        if g == 0:
                            wt(t, t_w3q[q])
                    pidx = idx - 4
                    if pidx >= 0:
                        t.wait_ge(vm.h, t_4ev[(q, pidx)])
                    elif q >= 1:
                        t.wait_ge(vm.h, t_4ev[(q - 1, 12 + idx)])
                    for K in range(MF):
                        mmi = t.matmul(ps_a[bank][:, :], qhT[:, g % 2, K * 256 + mi * 128:K * 256 + mi * 128 + 128],
                                       w3q[:, q % 2, K * 512:(K + 1) * 512],
                                       start=(K == 0), stop=(K == MF - 1))
                    t_4mm[(q, idx)] = t_inc(mmi, tm)

                @blk.vector
                def _(v, q=q, idx=idx, bank=bank, b=b, csum=csum, csq=csq):
                    v.wait_ge(tm.h, t_4mm[(q, idx)])
                    v.drain()
                    if idx >= 2:
                        wt(v, t_4sp[(q, idx - 2)])
                    elif q >= 1:
                        wt(v, t_4sp[(q - 1, 14 + idx)])
                    v.tensor_scalar(out=pre4[:, b, :], in0=ps_a[bank][:, :], scalar1=wscale[:, 2:3], scalar2=0.0,
                                    op0=ALU.mult, op1=ALU.add, accum_out=ln4st[:, csum:csum + 1])
                    v.drain()
                    v.scalar_tensor_tensor(out=ps_dmp[:, :], in0=pre4[:, b, :], scalar=0.0, in1=pre4[:, b, :],
                                           op0=ALU.subtract, op1=ALU.mult, accum_out=ln4st[:, csq:csq + 1])
                    t_4ev[(q, idx)] = t_inc(v.engine_nop(), vm)

                @blk.sync
                def _(sy, q=q, m=m, idx=idx, b=b):
                    sy.wait_ge(vm.h, t_4ev[(q, idx)])
                    t_4sp[(q, idx)] = dma16(sy, SXP, out=D_pre4[m * 128:(m + 1) * 128, q * 512:(q + 1) * 512],
                                            in_=pre4[:, b, :])

            if mi == 1 and g + 2 < 8:
                emit_qhT_load(g + 2)

    barrier()
    if upto <= 5:
        e4.close(); ctx.close()
        return nc
    e4.close()

    e5 = contextlib.ExitStack()
    p4_b = e5.enter_context(sb("p4_b", [128, 2, D], BF16))
    x2_b = e5.enter_context(sb("x2_b", [128, 2, D], F32))
    o_f = e5.enter_context(sb("o_f", [128, 2, D], F32))
    st5 = e5.enter_context(sb("st5", [128, 16], F32))

    t_5ld, t_5c, t_5st = {}, {}, {}

    for m in range(MT):
        o = (m % 2) * 8

        @blk.sync
        def _(sy, m=m):
            if m >= 2:
                sy.wait_ge(vm.h, t_5c[m - 2])
            tok_a = dma16(sy, SXP, out=p4_b[:, m % 2, :], in_=D_pre4[m * 128:(m + 1) * 128, :])
            tok_b = dma16(sy, SXP, out=x2_b[:, m % 2, :], in_=D_x2[m * 128:(m + 1) * 128, :])
            t_5ld[m] = [tok_a, tok_b]

        @blk.vector
        def _(v, m=m, o=o):
            wt(v, t_5ld[m])
            v.drain()
            if m >= 2:
                wt(v, t_5st[m - 2])
            v.tensor_reduce(out=st5[:, o:o + 1], in_=ln4st[:, 8 * m:8 * m + 4], axis=AX.X, op=ALU.add)
            v.tensor_reduce(out=st5[:, o + 1:o + 2], in_=ln4st[:, 8 * m + 4:8 * m + 8], axis=AX.X, op=ALU.add)
            v.drain()
            v.tensor_scalar_mul(out=st5[:, o:o + 1], in0=st5[:, o:o + 1], scalar1=1.0 / D)
            v.tensor_scalar_mul(out=st5[:, o + 1:o + 2], in0=st5[:, o + 1:o + 2], scalar1=1.0 / D)
            v.drain()
            v.tensor_tensor(out=st5[:, o + 2:o + 3], in0=st5[:, o:o + 1], in1=st5[:, o:o + 1], op=ALU.mult)
            v.drain()
            t_inc(v.tensor_scalar(out=st5[:, o + 3:o + 4], in0=st5[:, o + 1:o + 2], scalar1=st5[:, o + 2:o + 3], scalar2=EPS_LN, op0=ALU.subtract, op1=ALU.add), vm)
        t_5a = vm.n

        def f_5(s, o=o, t_5a=t_5a):
            s.wait_ge(vm.h, t_5a)
            s.activation(out=st5[:, o + 4:o + 5], in_=st5[:, o + 3:o + 4], func=AF.Ln)
            s.drain()
            s.activation(out=st5[:, o + 5:o + 6], in_=st5[:, o + 4:o + 5], func=AF.Exp, scale=epsc[:, 2:3]).then_inc(am.h, 1)
        t_5b = sadd(f_5)

        @blk.vector
        def _(v, m=m, o=o, t_5b=t_5b):
            v.wait_ge(am.h, t_5b)
            v.drain()
            v.tensor_scalar(out=o_f[:, m % 2, :], in0=p4_b[:, m % 2, :], scalar1=st5[:, o:o + 1], scalar2=st5[:, o + 5:o + 6], op0=ALU.subtract, op1=ALU.mult)
            v.drain()
            t_5c[m] = t_inc(v.tensor_tensor(out=o_f[:, m % 2, :], in0=o_f[:, m % 2, :], in1=x2_b[:, m % 2, :], op=ALU.add), vm)

        @blk.sync
        def _(sy, m=m):
            sy.wait_ge(vm.h, t_5c[m])
            t_5st[m] = dma16(sy, SXP, out=out[m * 128:(m + 1) * 128, :], in_=o_f[:, m % 2, :])

    barrier()
    e5.close()
    ctx.close()
    return nc


_NC_CACHE = {}


def _get_nc(debug=False):
    key = bool(debug)
    if key not in _NC_CACHE:
        _NC_CACHE[key] = build_nc(debug=key)
    return _NC_CACHE[key]


def make_in_maps(L0_attn_x, L0_attn_slope_rate, L0_attn_W_qkvu, L0_attn_W_out,
                 L0_ffn_w1, L0_ffn_w2, L0_ffn_w3):
    x = np.asarray(L0_attn_x, dtype=np.float32)
    Wqkvu = np.asarray(L0_attn_W_qkvu, dtype=np.float32)
    wqkT_a = np.ascontiguousarray(Wqkvu[0:2 * D, :].T)
    wvuT_a = np.ascontiguousarray(Wqkvu[2 * D:4 * D, :].T)
    woT_a = np.ascontiguousarray(np.asarray(L0_attn_W_out, dtype=np.float32).T)
    w1T_a = np.ascontiguousarray(np.asarray(L0_ffn_w1, dtype=np.float32).T)
    w2T_a = np.ascontiguousarray(np.asarray(L0_ffn_w2, dtype=np.float32).T)
    w3T_a = np.ascontiguousarray(np.asarray(L0_ffn_w3, dtype=np.float32).T)
    slope = np.asarray(L0_attn_slope_rate, dtype=np.float32).reshape(H)
    slope_ba = np.ascontiguousarray(np.broadcast_to(slope[None, :], (128, H)))

    in_maps = []
    for c in range(NC):
        b, half = c // 2, c % 2
        xs = np.ascontiguousarray(x[b, half * T:(half + 1) * T, :])
        mask = np.zeros((128, NC), np.float32)
        if half == 1:
            mask[:, c - 1] = 1.0
        in_maps.append({
            "x_in": xs, "wqkT": wqkT_a, "wvuT": wvuT_a, "woT": woT_a,
            "w1T": w1T_a, "w2T": w2T_a, "w3T": w3T_a,
            "slope_b": slope_ba, "kvmask": np.ascontiguousarray(mask),
        })
    return in_maps


def kernel(L0_attn_x, L0_attn_slope_rate, L0_attn_W_qkvu, L0_attn_W_out,
           L0_ffn_w1, L0_ffn_w2, L0_ffn_w3):
    in_maps = make_in_maps(L0_attn_x, L0_attn_slope_rate, L0_attn_W_qkvu, L0_attn_W_out,
                           L0_ffn_w1, L0_ffn_w2, L0_ffn_w3)
    nc = _get_nc(debug=False)
    res = run_bass_kernel_spmd(nc, in_maps, core_ids=list(range(NC)))
    outp = np.empty((4, 4096, D), dtype=np.float32)
    for c in range(NC):
        b, half = c // 2, c % 2
        outp[b, half * T:(half + 1) * T, :] = res.results[c]["out"]
    return outp


if __name__ == "__main__":
    nco = build_nc()
    print("built ok; instructions:", nco.n_instructions())
    print("sbuf bytes remaining/partition:", nco.sbuf_bytes_remaining)
